# version 02
# baseline (speedup 1.0000x reference)
import numpy as np
import ml_dtypes
from contextlib import ExitStack
import concourse.bass as bass
import concourse.mybir as mybir
from concourse.bass_utils import run_bass_kernel_spmd

F32 = mybir.dt.float32
BF16 = mybir.dt.bfloat16
I32 = mybir.dt.int32
I8 = mybir.dt.int8
I16 = mybir.dt.int16
AF = mybir.ActivationFunctionType
ALU = mybir.AluOpType
AX = mybir.AxisListType
BF = ml_dtypes.bfloat16

NC = 8
L = 8192
D = 2048
R = L // NC
KT = D // 128
EPS = 1e-5
CH = R // 512

_DMA_NAMES = {"dma_start", "indirect_dma_start", "collective_compute", "dma_start_transpose"}


class CEng:

    def __init__(self, eng, sem, auto=True):
        self._e = eng
        self._s = sem
        self.n = 0
        self._auto = auto

    def wait_ge(self, sem, val):
        self._e.wait_ge(sem, val)

    def mark(self, ins):
        ins.then_inc(self._s, 1)
        self.n += 1
        return self.n

    def __getattr__(self, name):
        f = getattr(self._e, name)
        if not callable(f):
            return f
        if name in _DMA_NAMES:
            def g(*a, **k):
                if self._auto and self.n:
                    self._e.wait_ge(self._s, self.n)
                return f(*a, **k)
            return g

        def g(*a, **k):
            if self._auto and self.n:
                self._e.wait_ge(self._s, self.n)
            ins = f(*a, **k)
            if self._auto:
                ins.then_inc(self._s, 1)
                self.n += 1
            return ins
        return g


class Marks:

    def __init__(self, prev=None):
        self.prev = prev or {}
        self.cur = {}

    def set(self, key, val):
        self.cur[key] = val

    def get(self, key):
        return self.prev.get(key, 0)


def tile_w(w):
    ind, outd = w.shape
    return np.ascontiguousarray(w.reshape(ind, outd // 128, 128).transpose(1, 0, 2))


def rope_tables():
    d = 128
    inv_freq = 1.0 / 10000.0 ** (np.arange(0, d, 2, dtype=np.float32) / d)
    t = np.arange(L, dtype=np.float32)
    freqs = np.outer(t, inv_freq)
    emb = np.concatenate([freqs, freqs], axis=-1)
    return np.cos(emb).astype(np.float32), np.sin(emb).astype(np.float32)



def build_pre_once(mk):
    nc = bass.Bass("TRN2", target_bir_lowering=False, debug=False)
    xt = nc.dram_tensor("xt", [D, R], F32, kind="ExternalInput").ap()
    wqkvg = nc.dram_tensor("wqkvg", [4, KT, D, 128], F32, kind="ExternalInput").ap()
    cos_t = nc.dram_tensor("cos_t", [128, R], F32, kind="ExternalInput").ap()
    sin_t = nc.dram_tensor("sin_t", [128, R], F32, kind="ExternalInput").ap()
    xn_o = nc.dram_tensor("xn_o", [D, R], BF16, kind="ExternalOutput").ap()
    q_o = nc.dram_tensor("q_o", [D, R], BF16, kind="ExternalOutput").ap()
    k_o = nc.dram_tensor("k_o", [D, R], BF16, kind="ExternalOutput").ap()
    v_o = nc.dram_tensor("v_o", [D, R], BF16, kind="ExternalOutput").ap()
    g_o = nc.dram_tensor("g_o", [D, R], BF16, kind="ExternalOutput").ap()
    outs = [q_o, k_o, v_o, g_o]

    with ExitStack() as ctx:
        e = ctx.enter_context
        xbf = e(nc.sbuf_tensor("xbf", [128, KT * R], BF16))
        xn = e(nc.sbuf_tensor("xn", [128, KT * R], BF16))
        sq = e(nc.sbuf_tensor("sq", [128, 2 * R], BF16))
        stat = e(nc.sbuf_tensor("stat", [1, 8 * R], F32))
        statb = e(nc.sbuf_tensor("statb", [1, 2 * R], BF16))
        rstd_b = e(nc.sbuf_tensor("rstd_b", [128, R], BF16))
        mrs_b = e(nc.sbuf_tensor("mrs_b", [128, R], BF16))
        ones_r = e(nc.sbuf_tensor("ones_r", [1, 128], BF16))
        ones_c = e(nc.sbuf_tensor("ones_c", [128, 1], BF16))
        cos_s = e(nc.sbuf_tensor("cos_s", [128, R], F32))
        sin_s = e(nc.sbuf_tensor("sin_s", [128, R], F32))
        wbuf = e(nc.sbuf_tensor("wbuf", [128, 3 * D], BF16))
        ev = e(nc.sbuf_tensor("ev", [128, 4 * R], F32))
        rs = e(nc.sbuf_tensor("rs", [128, 2 * R], F32))
        ob = e(nc.sbuf_tensor("ob", [128, 4 * R], BF16))
        pA = e(nc.psum_tensor("pA", [128, 1024], F32))
        pB = e(nc.psum_tensor("pB", [128, 1024], F32))
        pC = e(nc.psum_tensor("pC", [128, 1024], F32))
        pD = e(nc.psum_tensor("pD", [128, 1024], F32))
        PS = (pA, pB, pC, pD)

        s_ld = e(nc.semaphore("s_ld"))
        s_cp = e(nc.semaphore("s_cp"))
        s_w = [e(nc.semaphore(f"s_w{j}")) for j in range(3)]
        s_o1 = e(nc.semaphore("s_o1"))
        s_o2 = [e(nc.semaphore(f"s_o2{j}")) for j in range(4)]
        s_vch = e(nc.semaphore("s_vch"))
        s_sch = e(nc.semaphore("s_sch"))
        s_tch = e(nc.semaphore("s_tch"))
        block = e(nc.Block())

        tch_proj = lambda si: KT + 2 + si

        @block.sync
        def _(sy):
            sy.dma_start(out=cos_s[:], in_=cos_t).then_inc(s_ld, 16)
            sy.dma_start(out=sin_s[:], in_=sin_t).then_inc(s_ld, 16)

        @block.gpsimd
        def _(g):
            for i in range(KT):
                g.dma_start(out=xbf[:, i * R:(i + 1) * R],
                            in_=xt[i * 128:(i + 1) * 128, :]).then_inc(s_cp, 16)
            for si in range(64):
                p, c = divmod(si, KT)
                b = si % 3
                if si >= 3:
                    g.wait_ge(s_tch, tch_proj(si - 3))
                g.dma_start(
                    out=wbuf[:, b * D:(b + 1) * D].rearrange("p (t c) -> p t c", c=128),
                    in_=wqkvg[p, c].rearrange("(t p) c -> p t c", p=128)).then_inc(s_w[b], 16)

        @block.vector
        def _(v_):
            v = CEng(v_, s_vch)
            v.memset(ones_r[:], 1.0)
            v.memset(ones_c[:], 1.0)
            mk.set("ones", v.n)
            v.wait_ge(s_tch, KT)
            sv = lambda j: stat[0:1, j * R:(j + 1) * R]
            v.tensor_copy(out=sv(0), in_=pA[0:1, 0:R])
            v.tensor_copy(out=sv(1), in_=pB[0:1, 0:R])
            v.tensor_scalar(out=sv(2), in0=sv(0), scalar1=1.0 / D, scalar2=None, op0=ALU.mult)
            v.tensor_mul(out=sv(3), in0=sv(2), in1=sv(2))
            v.scalar_tensor_tensor(out=sv(4), in0=sv(1), scalar=1.0 / D, in1=sv(3),
                                   op0=ALU.mult, op1=ALU.subtract)
            v.tensor_scalar(out=sv(5), in0=sv(4), scalar1=1.0, scalar2=EPS,
                            op0=ALU.mult, op1=ALU.add)
            v.reciprocal(out=sv(6), in_=sv(5))
            mk.set("recip", v.n)
            v.wait_ge(s_sch, mk.get("sqrt"))
            v.tensor_mul(out=sv(3), in0=sv(2), in1=sv(7))
            v.tensor_copy(out=statb[0:1, 0:R], in_=sv(7))
            v.tensor_copy(out=statb[0:1, R:2 * R], in_=sv(3))
            mk.set("statb", v.n)
            v.wait_ge(s_sch, mk.get("bcast_ev"))
            for i in range(KT):
                t0 = xn[:, i * R:(i + 1) * R]
                v.tensor_mul(out=t0, in0=xbf[:, i * R:(i + 1) * R], in1=rstd_b[:])
                v.tensor_sub(out=t0, in0=t0, in1=mrs_b[:])
                mk.set(f"xn{i}", v.n)
            v.wait_ge(s_ld, 32)
            for si in range(2 * KT):
                eoff = (si % 4) * R
                roff = (si % 2) * R
                v.wait_ge(s_sch, mk.get(f"ev{si}"))
                if si >= 4:
                    v.wait_ge(s_o2[si % 4], 16 * (si // 4))
                v.tensor_scalar(out=rs[0:64, roff:roff + R], in0=ev[64:128, eoff:eoff + R],
                                scalar1=-1.0, scalar2=None, op0=ALU.mult)
                v.tensor_copy(out=rs[64:128, roff:roff + R], in_=ev[0:64, eoff:eoff + R])
                v.tensor_mul(out=rs[:, roff:roff + R], in0=rs[:, roff:roff + R], in1=sin_s[:])
                v.tensor_mul(out=ev[:, eoff:eoff + R], in0=ev[:, eoff:eoff + R], in1=cos_s[:])
                v.tensor_add(out=ob[:, eoff:eoff + R], in0=ev[:, eoff:eoff + R],
                             in1=rs[:, roff:roff + R])
                mk.set(f"rope{si}", v.n)

        @block.scalar
        def _(s_):
            s = CEng(s_, s_sch)
            s.wait_ge(s_cp, 16 * KT)
            for i in range(KT):
                if i >= 2:
                    s.wait_ge(s_tch, i - 1)
                s.activation(out=sq[:, (i % 2) * R:((i % 2) + 1) * R],
                             in_=xbf[:, i * R:(i + 1) * R], func=AF.Square)
                mk.set(f"sq{i}", s.n)
            s.wait_ge(s_vch, mk.get("recip"))
            s.activation(out=stat[0:1, 7 * R:8 * R], in_=stat[0:1, 6 * R:7 * R], func=AF.Sqrt)
            mk.set("sqrt", s.n)
            s.wait_ge(s_tch, KT + 1)
            s.activation(out=rstd_b[:], in_=pC[:, 0:R], func=AF.Identity)
            s.activation(out=mrs_b[:], in_=pD[:, 0:R], func=AF.Identity)
            mk.set("bcast_ev", s.n)
            for si in range(64):
                p, c = divmod(si, KT)
                s.wait_ge(s_tch, tch_proj(si))
                psrc = PS[si % 4]
                if p <= 1:
                    if si >= 4:
                        s.wait_ge(s_vch, mk.get(f"rope{si - 4}"))
                    s.activation(out=ev[:, (si % 4) * R:((si % 4) + 1) * R],
                                 in_=psrc[:, 0:R], func=AF.Identity)
                else:
                    if si >= 4:
                        s.wait_ge(s_o2[si % 4], 16 * (si // 4))
                    s.activation(out=ob[:, (si % 4) * R:((si % 4) + 1) * R],
                                 in_=psrc[:, 0:R], func=AF.Identity)
                mk.set(f"ev{si}", s.n)

        @block.tensor
        def _(t_):
            t = CEng(t_, s_tch, auto=False)
            t.wait_ge(s_vch, mk.get("ones"))
            t.wait_ge(s_cp, 16 * KT)
            for i in range(KT):
                t.wait_ge(s_sch, mk.get(f"sq{i}"))
                for ch in range(CH):
                    t.matmul(pA[0:1, ch * 512:(ch + 1) * 512], ones_c[:],
                             xbf[:, i * R + ch * 512:i * R + (ch + 1) * 512],
                             start=(i == 0), stop=(i == KT - 1))
                    ins = t.matmul(pB[0:1, ch * 512:(ch + 1) * 512], ones_c[:],
                                   sq[:, (i % 2) * R + ch * 512:(i % 2) * R + (ch + 1) * 512],
                                   start=(i == 0), stop=(i == KT - 1))
                t.mark(ins)
            t.wait_ge(s_vch, mk.get("statb"))
            for ch in range(CH):
                t.matmul(pC[:, ch * 512:(ch + 1) * 512], ones_r[:],
                         statb[0:1, ch * 512:(ch + 1) * 512], start=True, stop=True)
                ins = t.matmul(pD[:, ch * 512:(ch + 1) * 512], ones_r[:],
                               statb[0:1, R + ch * 512:R + (ch + 1) * 512], start=True, stop=True)
            t.mark(ins)
            t.wait_ge(s_vch, mk.get(f"xn{KT - 1}"))
            for si in range(64):
                b = si % 3
                pdst = PS[si % 4]
                t.wait_ge(s_w[b], 16 * (si // 3 + 1))
                if si >= 4:
                    t.wait_ge(s_sch, mk.get(f"ev{si - 4}"))
                for k in range(KT):
                    wt = wbuf[:, b * D + k * 128:b * D + (k + 1) * 128]
                    for ch in range(CH):
                        ins = t.matmul(pdst[:, ch * 512:(ch + 1) * 512], wt,
                                       xn[:, k * R + ch * 512:k * R + (ch + 1) * 512],
                                       start=(k == 0), stop=(k == KT - 1))
                t.mark(ins)

        @block.sync
        def _(sy):
            for i in range(KT):
                sy.wait_ge(s_vch, mk.get(f"xn{i}"))
                sy.dma_start(out=xn_o[i * 128:(i + 1) * 128, :],
                             in_=xn[:, i * R:(i + 1) * R]).then_inc(s_o1, 16)
            for si in range(64):
                p, c = divmod(si, KT)
                if p <= 1:
                    sy.wait_ge(s_vch, mk.get(f"rope{si}"))
                else:
                    sy.wait_ge(s_sch, mk.get(f"ev{si}"))
                sy.dma_start(out=outs[p][c * 128:(c + 1) * 128, :],
                             in_=ob[:, (si % 4) * R:((si % 4) + 1) * R]).then_inc(s_o2[si % 4], 16)
            sy.wait_ge(s_o1, 16 * KT)
            for j in range(4):
                sy.wait_ge(s_o2[j], 16 * 16)

    return nc


def build_twice(builder):
    mk = Marks()
    builder(mk)
    mk2 = Marks(prev=mk.cur)
    nc = builder(mk2)
    for key, val in mk2.cur.items():
        assert mk.cur[key] == val, f"mark {key} changed between passes"
    return nc


def host_prep_pre(inputs):
    x = np.asarray(inputs["L0_attn_hidden_states"])[0]
    xt = np.ascontiguousarray(x.T)
    wqkvg = np.stack([tile_w(np.ascontiguousarray(np.asarray(inputs[k]).T))
                      for k in ("L0_attn_wq", "L0_attn_wk", "L0_attn_wv", "L0_attn_wg")])
    cos, sin = rope_tables()
    cos_t = np.ascontiguousarray(cos.T)
    sin_t = np.ascontiguousarray(sin.T)
    maps = []
    for i in range(NC):
        sl = slice(i * R, (i + 1) * R)
        maps.append({
            "xt": np.ascontiguousarray(xt[:, sl]),
            "wqkvg": wqkvg,
            "cos_t": np.ascontiguousarray(cos_t[:, sl]),
            "sin_t": np.ascontiguousarray(sin_t[:, sl]),
        })
    return maps


def kernel(**inputs):
    cores = list(range(NC))
    r1 = run_bass_kernel_spmd(build_twice(build_pre_once), host_prep_pre(inputs), core_ids=cores).results
    r2 = run_bass_kernel_spmd(build_twice(build_attn_once), host_prep_attn(r1), core_ids=cores).results
    r3 = run_bass_kernel_spmd(build_twice(build_epi_once), host_prep_epi(inputs, r1, r2), core_ids=cores).results
    r4 = run_bass_kernel_spmd(build_twice(build_dq_once), host_prep_dq(inputs), core_ids=cores).results
    Wg, Wu, Wd = host_post_dq(r4)
    r5 = run_bass_kernel_spmd(build_twice(build_ffn_once), host_prep_ffn(inputs, r1, r3, Wg, Wu, Wd),
                              core_ids=cores).results
    x1_shards = [np.ascontiguousarray(np.asarray(r["x1_o"])) for r in r5]
    r6 = run_bass_kernel_spmd(build_twice(build_l6_once), host_prep_l6(inputs, x1_shards),
                              core_ids=cores).results
    r7 = run_bass_kernel_spmd(build_twice(build_mm_once), host_prep_mm(inputs, r6), core_ids=cores).results
    r8a = run_bass_kernel_spmd(
        build_twice(build_cmb_once),
        host_prep_cmb(inputs,
                      [np.asarray(r["zx_o"]) for r in r6],
                      [np.asarray(r["y_o"]) for r in r7],
                      [np.asarray(r["st_o"]) for r in r7],
                      [np.asarray(r["rc_o"]) for r in r7],
                      [np.asarray(r["cs_o"]) for r in r7]),
        core_ids=cores).results
    r8b = run_bass_kernel_spmd(
        build_twice(build_fin_once),
        host_prep_fin(inputs,
                      [np.asarray(r["yn_o"]) for r in r8a],
                      [np.asarray(r["xn1_o"]) for r in r6],
                      x1_shards,
                      [np.asarray(r["acc1_o"]) for r in r6]),
        core_ids=cores).results
    x2 = np.concatenate([np.asarray(r["x2_o"], dtype=np.float32).T for r in r8b], axis=0)
    return x2.reshape(1, L, D)

SCALE = 128 ** -0.5
NT = L // 512


def build_attn_once(mk):
    nc = bass.Bass("TRN2", target_bir_lowering=False, debug=False)
    qh = nc.dram_tensor("qh", [2, 128, L], BF16, kind="ExternalInput").ap()
    kh = nc.dram_tensor("kh", [2, 128, L], BF16, kind="ExternalInput").ap()
    vh = nc.dram_tensor("vh", [2, 128, L], BF16, kind="ExternalInput").ap()
    msk = nc.dram_tensor("msk", [128, 4 * 512], BF16, kind="ExternalInput").ap()
    idm = nc.dram_tensor("idm", [128, 128], BF16, kind="ExternalInput").ap()
    oh = nc.dram_tensor("oh", [2, 128, L], F32, kind="ExternalOutput").ap()
    den_o = nc.dram_tensor("den_o", [2, NT, 1, 512], F32, kind="ExternalOutput").ap()

    with ExitStack() as ctx:
        e = ctx.enter_context
        qs = e(nc.sbuf_tensor("qs", [128, 2 * L], BF16))
        ks = e(nc.sbuf_tensor("ks", [128, 2 * L], BF16))
        vs = e(nc.sbuf_tensor("vs", [128, 2 * L], BF16))
        vt = e(nc.sbuf_tensor("vt", [128, 2 * L], BF16))
        pbuf = e(nc.sbuf_tensor("pbuf", [128, 2 * 1024], BF16))
        mskb = e(nc.sbuf_tensor("mskb", [128, 4 * 512], BF16))
        ident = e(nc.sbuf_tensor("ident", [128, 128], BF16))
        ones_c = e(nc.sbuf_tensor("ones_c", [128, 1], BF16))
        ost = e(nc.sbuf_tensor("ost", [128, 2 * 512], F32))
        dst = e(nc.sbuf_tensor("dst", [1, 2 * 512], F32))
        psA = e(nc.psum_tensor("psA", [128, 1024], F32))
        psB = e(nc.psum_tensor("psB", [128, 1024], F32))
        psY = e(nc.psum_tensor("psY", [128, 512], F32))
        psD = e(nc.psum_tensor("psD", [128, 512], F32))
        psT = e(nc.psum_tensor("psT", [128, 1024], BF16))

        s_ld = e(nc.semaphore("s_ld"))
        s_oo = [e(nc.semaphore(f"s_oo{j}")) for j in range(2)]
        s_vch = e(nc.semaphore("s_vch"))
        s_sch = e(nc.semaphore("s_sch"))
        s_tch = e(nc.semaphore("s_tch"))
        block = e(nc.Block())

        @block.sync
        def _(sy):
            for h in range(2):
                sy.dma_start(out=qs[:, h * L:(h + 1) * L], in_=qh[h]).then_inc(s_ld, 16)
                sy.dma_start(out=ks[:, h * L:(h + 1) * L], in_=kh[h]).then_inc(s_ld, 16)
                sy.dma_start(out=vs[:, h * L:(h + 1) * L], in_=vh[h]).then_inc(s_ld, 16)
            sy.dma_start(out=mskb[:], in_=msk).then_inc(s_ld, 16)
            sy.dma_start(out=ident[:], in_=idm).then_inc(s_ld, 16)
        LD_ALL = 16 * 8

        @block.vector
        def _(v_):
            v = CEng(v_, s_vch)
            v.memset(ones_c[:], 1.0)
            mk.set("a_ones", v.n)
            for h in range(2):
                for t in range(NT):
                    for j in range(2):
                        hg = 2 * t + j
                        slot = hg % 2
                        v.wait_ge(s_sch, mk.get(f"a_exp{h}_{t}_{hg}"))
                        v.tensor_mul(out=pbuf[:, slot * 1024:(slot + 1) * 1024],
                                     in0=pbuf[:, slot * 1024:(slot + 1) * 1024],
                                     in1=mskb[:, j * 1024:(j + 1) * 1024])
                        mk.set(f"a_msk{h}_{t}_{j}", v.n)

        @block.scalar
        def _(s_):
            s = CEng(s_, s_sch)
            s.wait_ge(s_ld, LD_ALL)
            for h in range(2):
                for rnd in range(8):
                    s.wait_ge(s_tch, mk.get(f"a_vt{h}_{rnd}"))
                    s.activation(out=vt[:, h * L + rnd * 1024:h * L + (rnd + 1) * 1024],
                                 in_=psT[:], func=AF.Identity)
                    mk.set(f"a_vte{h}_{rnd}", s.n)
                for t in range(NT):
                    for hg in range(2 * t + 2):
                        slot = hg % 2
                        psrc = psA if slot == 0 else psB
                        s.wait_ge(s_tch, mk.get(f"a_sc{h}_{t}_{hg}"))
                        if hg - 2 >= 0:
                            s.wait_ge(s_tch, mk.get(f"a_pvp{h}_{t}_{hg - 2}"))
                        elif t > 0:
                            s.wait_ge(s_tch, mk.get(f"a_pvp{h}_{t - 1}_{2 * (t - 1) + slot}"))
                        elif h > 0:
                            s.wait_ge(s_tch, mk.get(f"a_pvp{h - 1}_{NT - 1}_{2 * (NT - 1) + slot}"))
                        s.activation(out=pbuf[:, slot * 1024:(slot + 1) * 1024],
                                     in_=psrc[:], func=AF.Exp, scale=SCALE)
                        mk.set(f"a_exp{h}_{t}_{hg}", s.n)
                    oslot = (h * NT + t) % 2
                    s.wait_ge(s_tch, mk.get(f"a_pv{h}_{t}"))
                    if h * NT + t >= 2:
                        s.wait_ge(s_oo[oslot], 16 * 2 * ((h * NT + t) // 2))
                    s.activation(out=ost[:, oslot * 512:(oslot + 1) * 512],
                                 in_=psY[:], func=AF.Identity)
                    s.activation(out=dst[0:1, oslot * 512:(oslot + 1) * 512],
                                 in_=psD[0:1, :], func=AF.Identity)
                    mk.set(f"a_oev{h}_{t}", s.n)

        @block.tensor
        def _(t_):
            t = CEng(t_, s_tch, auto=False)
            t.wait_ge(s_ld, LD_ALL)
            t.wait_ge(s_vch, mk.get("a_ones"))
            for h in range(2):
                for rnd in range(8):
                    if rnd >= 1:
                        t.wait_ge(s_sch, mk.get(f"a_vte{h}_{rnd - 1}"))
                    for j in range(8):
                        b = rnd * 8 + j
                        ins = t.transpose(psT[:, j * 128:(j + 1) * 128],
                                          vs[:, h * L + b * 128:h * L + (b + 1) * 128],
                                          ident[:])
                    t.mark(ins)
                    mk.set(f"a_vt{h}_{rnd}", t.n)
                for t_i in range(NT):
                    q_rhs = qs[:, h * L + t_i * 512:h * L + (t_i + 1) * 512]
                    nhg = 2 * t_i + 2

                    def emit_sc(hg):
                        slot = hg % 2
                        pdst = psA if slot == 0 else psB
                        prev = hg - 2
                        if prev >= 0:
                            t.wait_ge(s_sch, mk.get(f"a_exp{h}_{t_i}_{prev}"))
                        elif t_i > 0:
                            t.wait_ge(s_sch, mk.get(f"a_exp{h}_{t_i - 1}_{2 * (t_i - 1) + slot}"))
                        elif h > 0:
                            t.wait_ge(s_sch, mk.get(f"a_exp{h - 1}_{NT - 1}_{2 * (NT - 1) + slot}"))
                        for j in range(2):
                            b = 2 * hg + j
                            ins = t.matmul(pdst[:, j * 512:(j + 1) * 512],
                                           ks[:, h * L + b * 128:h * L + (b + 1) * 128],
                                           q_rhs, start=True, stop=True)
                        t.mark(ins)
                        mk.set(f"a_sc{h}_{t_i}_{hg}", t.n)

                    def emit_pv(hg):
                        slot = hg % 2
                        diag = hg >= 2 * t_i
                        if diag:
                            t.wait_ge(s_vch, mk.get(f"a_msk{h}_{t_i}_{hg - 2 * t_i}"))
                        else:
                            t.wait_ge(s_sch, mk.get(f"a_exp{h}_{t_i}_{hg}"))
                        if hg == 0 and (h * NT + t_i) >= 1:
                            prev_t = (h, t_i - 1) if t_i > 0 else (h - 1, NT - 1)
                            t.wait_ge(s_sch, mk.get(f"a_oev{prev_t[0]}_{prev_t[1]}"))
                        for j in range(2):
                            b = 2 * hg + j
                            first = (b == 0)
                            last = (b == 4 * t_i + 3)
                            t.matmul(psY[:], vt[:, h * L + b * 128:h * L + (b + 1) * 128],
                                     pbuf[:, slot * 1024 + j * 512:slot * 1024 + (j + 1) * 512],
                                     start=first, stop=last)
                            ins = t.matmul(psD[0:1, :], ones_c[:],
                                           pbuf[:, slot * 1024 + j * 512:slot * 1024 + (j + 1) * 512],
                                           start=first, stop=last)
                        t.mark(ins)
                        mk.set(f"a_pvp{h}_{t_i}_{hg}", t.n)

                    for hg in range(nhg):
                        emit_sc(hg)
                        if hg >= 1:
                            emit_pv(hg - 1)
                    emit_pv(nhg - 1)
                    mk.set(f"a_pv{h}_{t_i}", t.n)

        @block.sync
        def _(sy):
            for h in range(2):
                for t_i in range(NT):
                    oslot = (h * NT + t_i) % 2
                    sy.wait_ge(s_sch, mk.get(f"a_oev{h}_{t_i}"))
                    sy.dma_start(out=oh[h][:, t_i * 512:(t_i + 1) * 512],
                                 in_=ost[:, oslot * 512:(oslot + 1) * 512]).then_inc(s_oo[oslot], 16)
                    sy.dma_start(out=den_o[h, t_i],
                                 in_=dst[0:1, oslot * 512:(oslot + 1) * 512]).then_inc(s_oo[oslot], 16)
            for j in range(2):
                sy.wait_ge(s_oo[j], 16 * 2 * (NT if j == 0 else NT))

    return nc


def attn_masks():
    m = np.zeros((128, 4, 512), dtype=BF)
    for j in range(4):
        kk = np.arange(128)[:, None]
        qq = np.arange(512)[None, :]
        m[:, j, :] = (qq >= j * 128 + kk).astype(np.float32)
    return np.ascontiguousarray(m.reshape(128, 4 * 512))


def host_prep_attn(pre_results):
    qf = np.concatenate([np.asarray(r["q_o"]) for r in pre_results], axis=1)
    kf = np.concatenate([np.asarray(r["k_o"]) for r in pre_results], axis=1)
    vf = np.concatenate([np.asarray(r["v_o"]) for r in pre_results], axis=1)
    msk = attn_masks()
    idm = np.eye(128, dtype=BF)
    maps = []
    for i in range(NC):
        sl = slice(i * 256, (i + 1) * 256)
        maps.append({
            "qh": np.ascontiguousarray(qf[sl]).reshape(2, 128, L),
            "kh": np.ascontiguousarray(kf[sl]).reshape(2, 128, L),
            "vh": np.ascontiguousarray(vf[sl]).reshape(2, 128, L),
            "msk": msk,
            "idm": idm,
        })
    return maps


def build_epi_once(mk):
    nc = bass.Bass("TRN2", target_bir_lowering=False, debug=False)
    oh_i = nc.dram_tensor("oh_i", [D, R], F32, kind="ExternalInput").ap()
    den_i = nc.dram_tensor("den_i", [R, 16], F32, kind="ExternalInput").ap()
    g_i = nc.dram_tensor("g_i", [D, R], BF16, kind="ExternalInput").ap()
    nw_i = nc.dram_tensor("nw_i", [1, 8 * 128], F32, kind="ExternalInput").ap()
    wo_t = nc.dram_tensor("wo_t", [KT, D, 128], F32, kind="ExternalInput").ap()
    idm = nc.dram_tensor("idm", [128, 128], BF16, kind="ExternalInput").ap()
    attn_o = nc.dram_tensor("attn_o", [D, R], F32, kind="ExternalOutput").ap()
    NH2 = 16

    with ExitStack() as ctx:
        e = ctx.enter_context
        obf = e(nc.sbuf_tensor("obf", [128, KT * R], BF16))
        gbf = e(nc.sbuf_tensor("gbf", [128, KT * R], BF16))
        otk = e(nc.sbuf_tensor("otk", [128, KT * R], BF16))
        gtk = e(nc.sbuf_tensor("gtk", [128, KT * R], BF16))
        ofT = e(nc.sbuf_tensor("ofT", [128, KT * R], BF16))
        dns = e(nc.sbuf_tensor("dns", [128, 8 * 16], F32))
        sA = e(nc.sbuf_tensor("sA", [128, 128], F32))
        fA = e(nc.sbuf_tensor("fA", [128, 4 * 128], F32))
        nwrb = e(nc.sbuf_tensor("nwrb", [1, 8 * 128], BF16))
        nwb8 = e(nc.sbuf_tensor("nwb8", [128, 1024], BF16))
        ident = e(nc.sbuf_tensor("ident", [128, 128], BF16))
        ones_r = e(nc.sbuf_tensor("ones_r", [1, 128], BF16))
        sqscrap = e(nc.sbuf_tensor("sqscrap", [128, 128], F32))
        wbuf = e(nc.sbuf_tensor("wbuf", [128, 3 * D], BF16))
        ost = e(nc.sbuf_tensor("ost", [128, 4 * R], F32))
        psT = e(nc.psum_tensor("psT", [128, 1024], BF16))
        pA = e(nc.psum_tensor("pA", [128, 1024], F32))
        pB = e(nc.psum_tensor("pB", [128, 1024], F32))
        pC = e(nc.psum_tensor("pC", [128, 1024], F32))
        PS3 = (pA, pB, pC)

        s_ld = e(nc.semaphore("s_ld"))
        s_lo = e(nc.semaphore("s_lo"))
        s_w = [e(nc.semaphore(f"s_w{j}")) for j in range(3)]
        s_oo = [e(nc.semaphore(f"s_oo{j}")) for j in range(4)]
        s_vch = e(nc.semaphore("s_vch"))
        s_sch = e(nc.semaphore("s_sch"))
        s_tch = e(nc.semaphore("s_tch"))
        block = e(nc.Block())

        @block.sync
        def _(sy):
            sy.dma_start(out=gbf[:].rearrange("p (t r) -> p t r", r=R),
                         in_=g_i.rearrange("(t p) r -> p t r", p=128)).then_inc(s_ld, 16)
            sy.dma_start(out=dns[:].rearrange("p (s h) -> p s h", h=16),
                         in_=den_i.rearrange("(s p) h -> p s h", p=128)).then_inc(s_ld, 16)
            sy.dma_start(out=ident[:], in_=idm).then_inc(s_ld, 16)

        @block.gpsimd
        def _(g):
            for i in range(KT):
                g.dma_start(out=obf[:, i * R:(i + 1) * R],
                            in_=oh_i[i * 128:(i + 1) * 128, :]).then_inc(s_lo, 16)
            g.dma_start(out=nwrb[:], in_=nw_i).then_inc(s_lo, 16)
            for si in range(KT):
                b = si % 3
                if si >= 3:
                    g.wait_ge(s_tch, mk.get(f"e_pj{si - 3}"))
                g.dma_start(
                    out=wbuf[:, b * D:(b + 1) * D].rearrange("p (t c) -> p t c", c=128),
                    in_=wo_t[si].rearrange("(t p) c -> p t c", p=128)).then_inc(s_w[b], 16)

        @block.tensor
        def _(t_):
            t = CEng(t_, s_tch, auto=False)
            t.wait_ge(s_ld, 48)
            t.wait_ge(s_lo, 16 * (KT + 1))
            t.wait_ge(s_vch, mk.get("e_ones"))
            for chn in range(2):
                ins = t.matmul(pC[:, chn * 512:(chn + 1) * 512], ones_r[:],
                               nwrb[0:1, chn * 512:(chn + 1) * 512], start=True, stop=True)
            t.mark(ins)
            mk.set("e_nwb", t.n)
            rnd = 0
            for h in range(NH2):
                for which, srcb in (("o", obf), ("g", gbf)):
                    if rnd >= 1:
                        t.wait_ge(s_sch, mk.get(f"e_vte{rnd - 1}"))
                    for j in range(8):
                        ins = t.transpose(psT[:, j * 128:(j + 1) * 128],
                                          srcb[:, h * R + j * 128:h * R + (j + 1) * 128],
                                          ident[:])
                    t.mark(ins)
                    mk.set(f"e_T_{which}{h}", t.n)
                    rnd += 1
            for h in range(NH2):
                t.wait_ge(s_vch, mk.get(f"e_fin{h}"))
                t.wait_ge(s_sch, mk.get(f"e_vte{rnd - 1}"))
                for j in range(8):
                    ins = t.transpose(psT[:, j * 128:(j + 1) * 128],
                                      otk[:, h * R + j * 128:h * R + (j + 1) * 128],
                                      ident[:])
                t.mark(ins)
                mk.set(f"e_Tb{h}", t.n)
                rnd += 1
            t.wait_ge(s_sch, mk.get(f"e_vte{rnd - 1}"))
            for si in range(KT):
                b = si % 3
                pdst = PS3[si % 3]
                t.wait_ge(s_w[b], 16 * (si // 3 + 1))
                if si >= 3:
                    t.wait_ge(s_sch, mk.get(f"e_oev{si - 3}"))
                for k in range(KT):
                    wt = wbuf[:, b * D + k * 128:b * D + (k + 1) * 128]
                    for chn in range(CH):
                        ins = t.matmul(pdst[:, chn * 512:(chn + 1) * 512], wt,
                                       ofT[:, k * R + chn * 512:k * R + (chn + 1) * 512],
                                       start=(k == 0), stop=(k == KT - 1))
                t.mark(ins)
                mk.set(f"e_pj{si}", t.n)

        @block.scalar
        def _(s_):
            s = CEng(s_, s_sch)
            nvte = 0
            s.wait_ge(s_tch, mk.get("e_nwb"))
            s.activation(out=nwb8[:], in_=pC[:], func=AF.Identity)
            mk.set("e_nwbe", s.n)
            for h in range(NH2):
                for which, dstb in (("o", otk), ("g", gtk)):
                    s.wait_ge(s_tch, mk.get(f"e_T_{which}{h}"))
                    s.activation(out=dstb[:, h * R:(h + 1) * R], in_=psT[:], func=AF.Identity)
                    mk.set(f"e_vte{nvte}", s.n)
                    nvte += 1
            for h in range(NH2):
                for j in range(8):
                    s.activation(out=sqscrap[:], in_=otk[:, h * R + j * 128:h * R + (j + 1) * 128],
                                 func=AF.Square, accum_out=sA[:, j * 16 + h:j * 16 + h + 1])
            mk.set("e_sq", s.n)
            for h in range(NH2):
                s.activation(out=gtk[:, h * R:(h + 1) * R], in_=gtk[:, h * R:(h + 1) * R],
                             func=AF.Silu)
            mk.set("e_silu", s.n)
            s.wait_ge(s_vch, mk.get("e_rcp"))
            s.activation(out=fA[:, 3 * 128:4 * 128], in_=fA[:, 2 * 128:3 * 128], func=AF.Sqrt)
            mk.set("e_f", s.n)
            for h in range(NH2):
                s.wait_ge(s_tch, mk.get(f"e_Tb{h}"))
                s.activation(out=ofT[:, h * R:(h + 1) * R], in_=psT[:], func=AF.Identity)
                mk.set(f"e_vte{nvte}", s.n)
                nvte += 1
            for si in range(KT):
                s.wait_ge(s_tch, mk.get(f"e_pj{si}"))
                oslot = si % 4
                if si >= 4:
                    s.wait_ge(s_oo[oslot], 16 * (si // 4))
                s.activation(out=ost[:, oslot * R:(oslot + 1) * R],
                             in_=PS3[si % 3][:, 0:R], func=AF.Identity)
                mk.set(f"e_oev{si}", s.n)

        @block.vector
        def _(v_):
            v = CEng(v_, s_vch)
            v.memset(ones_r[:], 1.0)
            mk.set("e_ones", v.n)
            v.wait_ge(s_sch, mk.get("e_sq"))
            v.wait_ge(s_ld, 48)
            dv = dns[:, 0:128]
            v.tensor_mul(out=fA[:, 0:128], in0=dv, in1=dv)
            v.tensor_scalar(out=fA[:, 1 * 128:2 * 128], in0=sA[:], scalar1=1.0 / 128, scalar2=None,
                            op0=ALU.mult)
            v.scalar_tensor_tensor(out=fA[:, 1 * 128:2 * 128], in0=fA[:, 0:128], scalar=EPS,
                                   in1=fA[:, 1 * 128:2 * 128], op0=ALU.mult, op1=ALU.add)
            v.reciprocal(out=fA[:, 2 * 128:3 * 128], in_=fA[:, 1 * 128:2 * 128])
            mk.set("e_rcp", v.n)
            v.wait_ge(s_sch, mk.get("e_f"))
            v.wait_ge(s_sch, mk.get("e_silu"))
            v.wait_ge(s_sch, mk.get("e_nwbe"))
            for h in range(NH2):
                for j in range(8):
                    sl = otk[:, h * R + j * 128:h * R + (j + 1) * 128]
                    v.tensor_scalar(out=sl, in0=sl, scalar1=fA[:, 3 * 128 + j * 16 + h:3 * 128 + j * 16 + h + 1],
                                    scalar2=None, op0=ALU.mult)
                v.tensor_mul(out=otk[:, h * R:(h + 1) * R], in0=otk[:, h * R:(h + 1) * R],
                             in1=nwb8[:])
                v.tensor_mul(out=otk[:, h * R:(h + 1) * R], in0=otk[:, h * R:(h + 1) * R],
                             in1=gtk[:, h * R:(h + 1) * R])
                mk.set(f"e_fin{h}", v.n)

        @block.sync
        def _(sy):
            for si in range(KT):
                sy.wait_ge(s_sch, mk.get(f"e_oev{si}"))
                sy.dma_start(out=attn_o[si * 128:(si + 1) * 128, :],
                             in_=ost[:, (si % 4) * R:((si % 4) + 1) * R]).then_inc(s_oo[si % 4], 16)
            for j in range(4):
                sy.wait_ge(s_oo[j], 16 * 4)

    return nc


def host_prep_epi(inputs, pre_results, attn_results):
    of = np.concatenate([np.asarray(r["oh"], np.float32).reshape(256, L)
                         for r in attn_results], axis=0)
    den = np.concatenate([np.asarray(r["den_o"], np.float32).reshape(2, L)
                          for r in attn_results], axis=0)
    nw = np.asarray(inputs["L0_attn_norm_w"], np.float32)
    nw8 = np.tile(nw, 8)[None, :]
    wo_tl = tile_w(np.ascontiguousarray(np.asarray(inputs["L0_attn_wo"]).T))
    idm = np.eye(128, dtype=BF)
    maps = []
    for i in range(NC):
        sl = slice(i * R, (i + 1) * R)
        maps.append({
            "oh_i": np.ascontiguousarray(of[:, sl]),
            "den_i": np.ascontiguousarray(den[:, sl].T),
            "g_i": np.ascontiguousarray(np.asarray(pre_results[i]["g_o"])),
            "nw_i": nw8,
            "wo_t": wo_tl,
            "idm": idm,
        })
    return maps

NDQ = 48


def build_dq_once(mk):
    nc = bass.Bass("TRN2", target_bir_lowering=False, debug=False)
    pk = nc.dram_tensor("pk", [NDQ, 128, 1024], I16, kind="ExternalInput").ap()
    sce = nc.dram_tensor("sce", [NDQ, 128, 2048], F32, kind="ExternalInput").ap()
    wdq = nc.dram_tensor("wdq", [NDQ, 128, 2048], BF16, kind="ExternalOutput").ap()
    with ExitStack() as ctx:
        e = ctx.enter_context
        pt = e(nc.sbuf_tensor("pt", [128, 2 * 1024], I16))
        scb = e(nc.sbuf_tensor("scb", [128, 2 * 2048], BF16))
        rr = e(nc.sbuf_tensor("rr", [128, 2048], I16))
        sb = e(nc.sbuf_tensor("sb", [128, 2048], I16))
        mz = e(nc.sbuf_tensor("mz", [128, 2048], I16))
        eq = e(nc.sbuf_tensor("eq", [128, 2048], I16))
        bits = e(nc.sbuf_tensor("bits", [128, 2048], I16))
        wb = e(nc.sbuf_tensor("wb", [128, 2 * 2048], BF16))
        s_lp = [e(nc.semaphore(f"s_lp{j}")) for j in range(2)]
        s_ls = [e(nc.semaphore(f"s_ls{j}")) for j in range(2)]
        s_ob = [e(nc.semaphore(f"s_ob{j}")) for j in range(2)]
        s_vch = e(nc.semaphore("s_vch"))
        block = e(nc.Block())

        @block.gpsimd
        def _(g):
            for i in range(NDQ):
                sl = i % 2
                if i >= 2:
                    g.wait_ge(s_vch, mk.get(f"dq_in{i - 2}"))
                g.dma_start(out=pt[:, sl * 1024:(sl + 1) * 1024], in_=pk[i]).then_inc(s_lp[sl], 16)
                if i >= 2:
                    g.wait_ge(s_vch, mk.get(f"dq_out{i - 2}"))
                g.dma_start(out=scb[:, sl * 2048:(sl + 1) * 2048], in_=sce[i]).then_inc(s_ls[sl], 16)

        @block.vector
        def _(v_):
            v = CEng(v_, s_vch)
            for i in range(NDQ):
                sl = i % 2
                v.wait_ge(s_lp[sl], 16 * (i // 2 + 1))
                v.wait_ge(s_ls[sl], 16 * (i // 2 + 1))
                if i >= 2:
                    v.wait_ge(s_ob[sl], 16 * (i // 2))
                p = pt[:, sl * 1024:(sl + 1) * 1024]
                v.tensor_single_scalar(out=rr[:, 0:1024], in_=p, scalar=7, op=ALU.bitwise_and)
                v.tensor_scalar(out=rr[:, 1024:2048], in0=p, scalar1=4, scalar2=7,
                                op0=ALU.logical_shift_right, op1=ALU.bitwise_and)
                v.tensor_scalar(out=sb[:, 0:1024], in0=p, scalar1=8, scalar2=12,
                                op0=ALU.bitwise_and, op1=ALU.logical_shift_left)
                v.tensor_scalar(out=sb[:, 1024:2048], in0=p, scalar1=128, scalar2=8,
                                op0=ALU.bitwise_and, op1=ALU.logical_shift_left)
                mk.set(f"dq_in{i}", v.n)
                v.tensor_single_scalar(out=mz[:], in_=rr[:], scalar=0, op=ALU.not_equal)
                v.tensor_single_scalar(out=mz[:], in_=mz[:], scalar=-1, op=ALU.mult)
                v.tensor_single_scalar(out=eq[:], in_=rr[:], scalar=1, op=ALU.is_equal)
                v.tensor_sub(out=rr[:], in0=rr[:], in1=eq[:])
                v.tensor_single_scalar(out=rr[:], in_=rr[:], scalar=252, op=ALU.add)
                v.tensor_single_scalar(out=bits[:], in_=rr[:], scalar=6, op=ALU.logical_shift_left)
                v.tensor_tensor(out=bits[:], in0=bits[:], in1=sb[:], op=ALU.bitwise_or)
                v.tensor_tensor(out=bits[:], in0=bits[:], in1=mz[:], op=ALU.bitwise_and)
                v.tensor_tensor(out=wb[:, sl * 2048:(sl + 1) * 2048], in0=bits[:].bitcast(BF16),
                                in1=scb[:, sl * 2048:(sl + 1) * 2048], op=ALU.mult)
                mk.set(f"dq_out{i}", v.n)

        @block.sync
        def _(sy):
            for i in range(NDQ):
                sl = i % 2
                sy.wait_ge(s_vch, mk.get(f"dq_out{i}"))
                sy.dma_start(out=wdq[i], in_=wb[:, sl * 2048:(sl + 1) * 2048]).then_inc(s_ob[sl], 16)
            for j in range(2):
                sy.wait_ge(s_ob[j], 16 * (NDQ // 2))
    return nc


def host_prep_dq(inputs):
    def tiles_gu(name_w, name_s):
        wp = np.asarray(inputs[name_w]).astype(np.int16)
        se = np.repeat(np.asarray(inputs[name_s], np.float32), 16, axis=1)
        return wp, se

    gW, gS = tiles_gu("L0_ffn_gate_W_packed", "L0_ffn_gate_scales")
    uW, uS = tiles_gu("L0_ffn_up_W_packed", "L0_ffn_up_scales")
    dW = np.asarray(inputs["L0_ffn_down_W_packed"]).astype(np.int16)
    dS = np.repeat(np.asarray(inputs["L0_ffn_down_scales"], np.float32), 16, axis=1)
    maps = []
    for i in range(NC):
        pks = np.empty((NDQ, 128, 1024), np.int16)
        scs = np.empty((NDQ, 128, 2048), np.float32)
        for t in range(16):
            rowsl = slice(i * 2048 + t * 128, i * 2048 + (t + 1) * 128)
            pks[t] = gW[rowsl]
            sc32 = np.repeat(np.asarray(inputs["L0_ffn_gate_scales"], np.float32), 32, axis=1)[rowsl]
            scs[t] = np.concatenate([sc32[:, 0::2], sc32[:, 1::2]], axis=1)
            pks[16 + t] = uW[rowsl]
            sc32u = np.repeat(np.asarray(inputs["L0_ffn_up_scales"], np.float32), 32, axis=1)[rowsl]
            scs[16 + t] = np.concatenate([sc32u[:, 0::2], sc32u[:, 1::2]], axis=1)
        for r2 in range(2):
            for c8 in range(8):
                rowsl = slice(i * 256 + r2 * 128, i * 256 + (r2 + 1) * 128)
                colsl = slice(c8 * 1024, (c8 + 1) * 1024)
                t = 32 + r2 * 8 + c8
                pks[t] = dW[rowsl, colsl]
                sc32d = np.repeat(np.asarray(inputs["L0_ffn_down_scales"], np.float32), 32, axis=1)[rowsl, c8 * 2048:(c8 + 1) * 2048]
                scs[t] = np.concatenate([sc32d[:, 0::2], sc32d[:, 1::2]], axis=1)
        maps.append({"pk": pks, "sce": scs})
    return maps


def host_post_dq(dq_results):
    Wg = np.empty((16384, 2048), BF)
    Wu = np.empty((16384, 2048), BF)
    Wd = np.empty((2048, 16384), BF)
    for i in range(NC):
        w = np.asarray(dq_results[i]["wdq"])
        for t in range(16):
            nat = np.empty((128, 2048), BF)
            nat[:, 0::2] = w[t][:, :1024]
            nat[:, 1::2] = w[t][:, 1024:]
            Wg[i * 2048 + t * 128:i * 2048 + (t + 1) * 128] = nat
            natu = np.empty((128, 2048), BF)
            natu[:, 0::2] = w[16 + t][:, :1024]
            natu[:, 1::2] = w[16 + t][:, 1024:]
            Wu[i * 2048 + t * 128:i * 2048 + (t + 1) * 128] = natu
        for r2 in range(2):
            for c8 in range(8):
                t = 32 + r2 * 8 + c8
                natd = np.empty((128, 2048), BF)
                natd[:, 0::2] = w[t][:, :1024]
                natd[:, 1::2] = w[t][:, 1024:]
                Wd[i * 256 + r2 * 128:i * 256 + (r2 + 1) * 128, c8 * 2048:(c8 + 1) * 2048] = natd
    return Wg, Wu, Wd

NCT = 128
NSB = 4
NHC = KT


def build_ffn_once(mk):
    nc = bass.Bass("TRN2", target_bir_lowering=False, debug=False)
    xn_i = nc.dram_tensor("xn_i", [D, R], BF16, kind="ExternalInput").ap()
    wg_t = nc.dram_tensor("wg_t", [NCT, D, 128], BF16, kind="ExternalInput").ap()
    wu_t = nc.dram_tensor("wu_t", [NCT, D, 128], BF16, kind="ExternalInput").ap()
    wd_t = nc.dram_tensor("wd_t", [NHC, 16384, 128], BF16, kind="ExternalInput").ap()
    xt_i = nc.dram_tensor("xt_i", [D, R], F32, kind="ExternalInput").ap()
    attn_i = nc.dram_tensor("attn_i", [D, R], F32, kind="ExternalInput").ap()
    x1_o = nc.dram_tensor("x1_o", [D, R], F32, kind="ExternalOutput").ap()
    RP = 512

    with ExitStack() as ctx:
        e = ctx.enter_context
        xn = e(nc.sbuf_tensor("xn", [128, KT * R], BF16))
        itb = e(nc.sbuf_tensor("itb", [128, 32 * RP], BF16))
        acc = e(nc.sbuf_tensor("acc", [128, NHC * RP], F32))
        wgu = e(nc.sbuf_tensor("wgu", [128, 6 * D], BF16))
        wdb = e(nc.sbuf_tensor("wdb", [128, 3 * 4096], BF16))
        g0 = e(nc.sbuf_tensor("g0", [128, 2 * RP], BF16))
        u0 = e(nc.sbuf_tensor("u0", [128, 2 * RP], BF16))
        tu = e(nc.sbuf_tensor("tu", [128, 2 * RP], BF16))
        tv = e(nc.sbuf_tensor("tv", [128, 2 * RP], BF16))
        xres = e(nc.sbuf_tensor("xres", [128, 2 * RP], F32))
        ares = e(nc.sbuf_tensor("ares", [128, 2 * RP], F32))
        ost = e(nc.sbuf_tensor("ost", [128, 2 * RP], F32))
        pG = e(nc.psum_tensor("pG", [128, 512], F32))
        pG2 = e(nc.psum_tensor("pG2", [128, 512], F32))
        pU = e(nc.psum_tensor("pU", [128, 512], F32))
        pU2 = e(nc.psum_tensor("pU2", [128, 512], F32))
        pD = e(nc.psum_tensor("pD", [128, 512], F32))
        pD2 = e(nc.psum_tensor("pD2", [128, 512], F32))
        PGs = (pG, pG2)
        PUs = (pU, pU2)
        PDs = (pD, pD2)

        s_ld = e(nc.semaphore("s_ld"))
        s_wg = [e(nc.semaphore(f"s_wg{j}")) for j in range(3)]
        s_wu = [e(nc.semaphore(f"s_wu{j}")) for j in range(3)]
        s_wd = [e(nc.semaphore(f"s_wd{j}")) for j in range(3)]
        s_xa = [e(nc.semaphore(f"s_xa{j}")) for j in range(2)]
        s_oo = [e(nc.semaphore(f"s_oo{j}")) for j in range(2)]
        s_vch = e(nc.semaphore("s_vch"))
        s_sch = e(nc.semaphore("s_sch"))
        s_tch = e(nc.semaphore("s_tch"))
        block = e(nc.Block())

        @block.sync
        def _(sy):
            for i in range(KT):
                sy.dma_start(out=xn[:, i * R:(i + 1) * R],
                             in_=xn_i[i * 128:(i + 1) * 128, :]).then_inc(s_ld, 16)

        @block.gpsimd
        def _(g):
            for rp in range(2):
                for sb in range(NSB):
                    for nt in range(32):
                        nct = sb * 32 + nt
                        gi = rp * NCT + nct
                        b = gi % 3
                        if gi >= 3:
                            g.wait_ge(s_tch, mk.get(f"f_gu{(gi - 3) // NCT}_{(gi - 3) % NCT}"))
                        g.dma_start(out=wgu[:, b * D:(b + 1) * D].rearrange("p (t c) -> p t c", c=128),
                                    in_=wg_t[nct].rearrange("(t p) c -> p t c", p=128)).then_inc(s_wg[b], 16)
                        g.dma_start(out=wgu[:, (3 + b) * D:(4 + b) * D].rearrange("p (t c) -> p t c", c=128),
                                    in_=wu_t[nct].rearrange("(t p) c -> p t c", p=128)).then_inc(s_wu[b], 16)
                    for hc in range(NHC):
                        di = (rp * NSB + sb) * NHC + hc
                        b = di % 3
                        if di >= 3:
                            pdi = di - 3
                            g.wait_ge(s_tch, mk.get(f"f_dn{pdi}"))
                        g.dma_start(out=wdb[:, b * 4096:(b + 1) * 4096].rearrange("p (t c) -> p t c", c=128),
                                    in_=wd_t[hc][sb * 4096:(sb + 1) * 4096, :].rearrange("(t p) c -> p t c", p=128)
                                    ).then_inc(s_wd[b], 16)
                for hc in range(NHC):
                    xi = rp * NHC + hc
                    sl = xi % 2
                    if xi >= 2:
                        g.wait_ge(s_vch, mk.get(f"f_x1{(xi - 2) // NHC}_{(xi - 2) % NHC}"))
                    g.dma_start(out=xres[:, sl * RP:(sl + 1) * RP],
                                in_=xt_i[hc * 128:(hc + 1) * 128, rp * RP:(rp + 1) * RP]).then_inc(s_xa[sl], 16)
                    g.dma_start(out=ares[:, sl * RP:(sl + 1) * RP],
                                in_=attn_i[hc * 128:(hc + 1) * 128, rp * RP:(rp + 1) * RP]).then_inc(s_xa[sl], 16)

        @block.tensor
        def _(t_):
            t = CEng(t_, s_tch, auto=False)
            t.wait_ge(s_ld, 16 * KT)
            for rp in range(2):
                for sb in range(NSB):
                    for nt in range(32):
                        nct = sb * 32 + nt
                        gi = rp * NCT + nct
                        b = gi % 3
                        sl = nct % 2
                        t.wait_ge(s_wg[b], 16 * (gi // 3 + 1))
                        t.wait_ge(s_wu[b], 16 * (gi // 3 + 1))
                        if gi >= 2:
                            pg = gi - 2
                            t.wait_ge(s_sch, mk.get(f"f_ge{pg // NCT}_{pg % NCT}"))
                        for k in range(KT):
                            t.matmul(PGs[sl][:], wgu[:, b * D + k * 128:b * D + (k + 1) * 128],
                                     xn[:, k * R + rp * RP:k * R + (rp + 1) * RP],
                                     start=(k == 0), stop=(k == KT - 1))
                        for k in range(KT):
                            ins = t.matmul(PUs[sl][:], wgu[:, (3 + b) * D + k * 128:(3 + b) * D + (k + 1) * 128],
                                           xn[:, k * R + rp * RP:k * R + (rp + 1) * RP],
                                           start=(k == 0), stop=(k == KT - 1))
                        t.mark(ins)
                        mk.set(f"f_gu{rp}_{nct}", t.n)
                    for hc in range(NHC):
                        di = (rp * NSB + sb) * NHC + hc
                        b = di % 3
                        sl = di % 2
                        t.wait_ge(s_wd[b], 16 * (di // 3 + 1))
                        t.wait_ge(s_vch, mk.get(f"f_it{rp}_{sb * 32 + 31}"))
                        if di >= 2:
                            t.wait_ge(s_vch, mk.get(f"f_acv{di - 2}"))
                            t.wait_ge(s_sch, mk.get(f"f_acs{di - 2}"))
                        for k in range(32):
                            ins = t.matmul(PDs[sl][:], wdb[:, b * 4096 + k * 128:b * 4096 + (k + 1) * 128],
                                           itb[:, k * RP:(k + 1) * RP],
                                           start=(k == 0), stop=(k == 31))
                        t.mark(ins)
                        mk.set(f"f_dn{di}", t.n)

        @block.scalar
        def _(s_):
            s = CEng(s_, s_sch)
            for rp in range(2):
                for sb in range(NSB):
                    for nt in range(32):
                        nct = sb * 32 + nt
                        sl = nct % 2
                        s.wait_ge(s_tch, mk.get(f"f_gu{rp}_{nct}"))
                        gi = rp * NCT + nct
                        if gi >= 2:
                            pg = gi - 2
                            s.wait_ge(s_vch, mk.get(f"f_it{pg // NCT}_{pg % NCT}"))
                        s.activation(out=g0[:, sl * RP:(sl + 1) * RP], in_=PGs[sl][:], func=AF.Identity)
                        s.activation(out=u0[:, sl * RP:(sl + 1) * RP], in_=PUs[sl][:], func=AF.Identity)
                        mk.set(f"f_ge{rp}_{nct}", s.n)
                        s.wait_ge(s_vch, mk.get(f"f_v2{rp}_{nct}"))
                        s.activation(out=tv[:, sl * RP:(sl + 1) * RP], in_=tv[:, sl * RP:(sl + 1) * RP],
                                     func=AF.Tanh, scale=0.7978845608028654)
                        mk.set(f"f_th{rp}_{nct}", s.n)
                    for hc in range(NHC):
                        di = (rp * NSB + sb) * NHC + hc
                        sl = di % 2
                        s.wait_ge(s_tch, mk.get(f"f_dn{di}"))
                        if sb == 0:
                            s.activation(out=acc[:, hc * RP:(hc + 1) * RP], in_=PDs[sl][:],
                                         func=AF.Identity)
                        mk.set(f"f_acs{di}", s.n)

        @block.vector
        def _(v_):
            v = CEng(v_, s_vch)
            for rp in range(2):
                for sb in range(NSB):
                    for nt in range(32):
                        nct = sb * 32 + nt
                        sl = nct % 2
                        S = lambda buf: buf[:, sl * RP:(sl + 1) * RP]
                        v.wait_ge(s_sch, mk.get(f"f_ge{rp}_{nct}"))
                        v.tensor_mul(out=S(tu), in0=S(g0), in1=S(g0))
                        v.tensor_scalar(out=S(tv), in0=S(tu), scalar1=0.044715, scalar2=1.0,
                                        op0=ALU.mult, op1=ALU.add)
                        v.tensor_mul(out=S(tv), in0=S(tv), in1=S(g0))
                        mk.set(f"f_v2{rp}_{nct}", v.n)
                        v.wait_ge(s_sch, mk.get(f"f_th{rp}_{nct}"))
                        v.tensor_scalar(out=S(tu), in0=S(tv), scalar1=1.0, scalar2=0.5,
                                        op0=ALU.add, op1=ALU.mult)
                        v.tensor_mul(out=S(tu), in0=S(tu), in1=S(g0))
                        if rp + sb > 0:
                            prev_last_di = (rp * NSB + sb - 1 + 1) * NHC - 1
                            v.wait_ge(s_tch, mk.get(f"f_dn{prev_last_di}"))
                        v.tensor_mul(out=itb[:, nt * RP:(nt + 1) * RP], in0=S(tu), in1=S(u0))
                        mk.set(f"f_it{rp}_{nct}", v.n)
                    for hc in range(NHC):
                        di = (rp * NSB + sb) * NHC + hc
                        sl = di % 2
                        if sb > 0:
                            v.wait_ge(s_tch, mk.get(f"f_dn{di}"))
                            v.tensor_add(out=acc[:, hc * RP:(hc + 1) * RP],
                                         in0=acc[:, hc * RP:(hc + 1) * RP], in1=PDs[sl][:])
                        else:
                            v.wait_ge(s_sch, mk.get(f"f_acs{di}"))
                        mk.set(f"f_acv{di}", v.n)
                for hc in range(NHC):
                    xi = rp * NHC + hc
                    sl = xi % 2
                    v.wait_ge(s_xa[sl], 32 * (xi // 2 + 1))
                    if xi >= 2:
                        v.wait_ge(s_oo[sl], 16 * (xi // 2))
                    v.tensor_add(out=ost[:, sl * RP:(sl + 1) * RP],
                                 in0=acc[:, hc * RP:(hc + 1) * RP], in1=xres[:, sl * RP:(sl + 1) * RP])
                    v.tensor_add(out=ost[:, sl * RP:(sl + 1) * RP],
                                 in0=ost[:, sl * RP:(sl + 1) * RP], in1=ares[:, sl * RP:(sl + 1) * RP])
                    mk.set(f"f_x1{rp}_{hc}", v.n)

        @block.sync
        def _(sy):
            for rp in range(2):
                for hc in range(NHC):
                    xi = rp * NHC + hc
                    sl = xi % 2
                    sy.wait_ge(s_vch, mk.get(f"f_x1{rp}_{hc}"))
                    sy.dma_start(out=x1_o[hc * 128:(hc + 1) * 128, rp * RP:(rp + 1) * RP],
                                 in_=ost[:, sl * RP:(sl + 1) * RP]).then_inc(s_oo[sl], 16)
            for j in range(2):
                sy.wait_ge(s_oo[j], 16 * NHC)

    return nc


def host_prep_ffn(inputs, pre_results, epi_results, Wg, Wu, Wd):
    wg_t = tile_w(np.ascontiguousarray(Wg.T))
    wu_t = tile_w(np.ascontiguousarray(Wu.T))
    wd_t = tile_w(np.ascontiguousarray(Wd.T))
    x = np.asarray(inputs["L0_attn_hidden_states"])[0]
    xt = np.ascontiguousarray(x.T)
    maps = []
    for i in range(NC):
        sl = slice(i * R, (i + 1) * R)
        maps.append({
            "xn_i": np.ascontiguousarray(np.asarray(pre_results[i]["xn_o"])),
            "wg_t": wg_t, "wu_t": wu_t, "wd_t": wd_t,
            "xt_i": np.ascontiguousarray(xt[:, sl]),
            "attn_i": np.ascontiguousarray(np.asarray(epi_results[i]["attn_o"])),
        })
    return maps

WCT = 80


def build_l6_once(mk):
    nc = bass.Bass("TRN2", target_bir_lowering=False, debug=False)
    x1_i = nc.dram_tensor("x1_i", [D, R], F32, kind="ExternalInput").ap()
    win_t = nc.dram_tensor("win_t", [WCT, D, 128], F32, kind="ExternalInput").ap()
    wdt_t = nc.dram_tensor("wdt_t", [D, 64], F32, kind="ExternalInput").ap()
    lnwb = nc.dram_tensor("lnwb", [128, 2 * KT], F32, kind="ExternalInput").ap()
    fc1_t = nc.dram_tensor("fc1_t", [64, D, 128], F32, kind="ExternalInput").ap()
    fc1b = nc.dram_tensor("fc1b", [128, 64], F32, kind="ExternalInput").ap()
    fc2_t = nc.dram_tensor("fc2_t", [KT, 8192, 128], F32, kind="ExternalInput").ap()
    fc2b = nc.dram_tensor("fc2b", [128, KT], F32, kind="ExternalInput").ap()
    xn1_o = nc.dram_tensor("xn1_o", [D, R], BF16, kind="ExternalOutput").ap()
    zx_o = nc.dram_tensor("zx_o", [WCT, 128, R], BF16, kind="ExternalOutput").ap()
    dt_o = nc.dram_tensor("dt_o", [64, R], BF16, kind="ExternalOutput").ap()
    acc1_o = nc.dram_tensor("acc1_o", [D, R], F32, kind="ExternalOutput").ap()
    RP = 512

    with ExitStack() as ctx:
        e = ctx.enter_context
        un1 = e(nc.sbuf_tensor("un1", [128, KT * R], BF16))
        xbf = un1
        itb = un1
        xn1 = e(nc.sbuf_tensor("xn1", [128, KT * R], BF16))
        hs = e(nc.sbuf_tensor("hs", [128, KT * R], BF16))
        acc = e(nc.sbuf_tensor("acc", [128, KT * RP], F32))
        sq = e(nc.sbuf_tensor("sq", [128, 2 * R], BF16))
        stat = e(nc.sbuf_tensor("stat", [1, 5 * R], F32))
        statb = e(nc.sbuf_tensor("statb", [1, 2 * R], BF16))
        rstd_b = e(nc.sbuf_tensor("rstd_b", [128, R], BF16))
        mrs_b = e(nc.sbuf_tensor("mrs_b", [128, R], BF16))
        lnwbs = e(nc.sbuf_tensor("lnwbs", [128, 2 * KT], F32))
        fc1bs = e(nc.sbuf_tensor("fc1bs", [128, 64], F32))
        fc2bs = e(nc.sbuf_tensor("fc2bs", [128, KT], F32))
        ones_r = e(nc.sbuf_tensor("ones_r", [1, 128], BF16))
        ones_c = e(nc.sbuf_tensor("ones_c", [128, 1], BF16))
        wbuf = e(nc.sbuf_tensor("wbuf", [128, 3 * D], BF16))
        wdtb = e(nc.sbuf_tensor("wdtb", [128, KT * 64], BF16))
        w2buf = e(nc.sbuf_tensor("w2buf", [128, 2 * 4096], BF16))
        g0 = e(nc.sbuf_tensor("g0", [128, 2 * RP], BF16))
        tu = e(nc.sbuf_tensor("tu", [128, 2 * RP], BF16))
        tv = e(nc.sbuf_tensor("tv", [128, 2 * RP], BF16))
        ost = e(nc.sbuf_tensor("ost", [128, 2 * R], BF16))
        ost2 = e(nc.sbuf_tensor("ost2", [128, 2 * RP], F32))
        pA = e(nc.psum_tensor("pA", [128, 1024], F32))
        pB = e(nc.psum_tensor("pB", [128, 1024], F32))
        pC = e(nc.psum_tensor("pC", [128, 1024], F32))
        pD = e(nc.psum_tensor("pD", [128, 1024], F32))
        PS = (pA, pB, pC, pD)

        s_ld = e(nc.semaphore("s_ld"))
        s_cp = e(nc.semaphore("s_cp"))
        s_w = [e(nc.semaphore(f"s_w{j}")) for j in range(3)]
        s_w2 = [e(nc.semaphore(f"s_w2{j}")) for j in range(2)]
        s_o1 = e(nc.semaphore("s_o1"))
        s_oz = [e(nc.semaphore(f"s_oz{j}")) for j in range(2)]
        s_oa = [e(nc.semaphore(f"s_oa{j}")) for j in range(2)]
        s_vch = e(nc.semaphore("s_vch"))
        s_sch = e(nc.semaphore("s_sch"))
        s_tch = e(nc.semaphore("s_tch"))
        block = e(nc.Block())

        def ln_stats_vector(v, li, src_tiles):
            sv = lambda j: stat[0:1, j * R:(j + 1) * R]
            v.wait_ge(s_tch, mk.get(f"l6_ln{li}_mm"))
            v.tensor_copy(out=sv(0), in_=pA[0:1, 0:R])
            v.tensor_copy(out=sv(1), in_=pB[0:1, 0:R])
            v.tensor_scalar(out=sv(2), in0=sv(0), scalar1=1.0 / D, scalar2=None, op0=ALU.mult)
            v.tensor_mul(out=sv(3), in0=sv(2), in1=sv(2))
            v.scalar_tensor_tensor(out=sv(4), in0=sv(1), scalar=1.0 / D, in1=sv(3),
                                   op0=ALU.mult, op1=ALU.subtract)
            v.tensor_scalar(out=sv(4), in0=sv(4), scalar1=1.0, scalar2=EPS,
                            op0=ALU.mult, op1=ALU.add)
            v.reciprocal(out=sv(3), in_=sv(4))
            mk.set(f"l6_rcp{li}", v.n)
            v.wait_ge(s_sch, mk.get(f"l6_sqrt{li}"))
            v.tensor_mul(out=sv(3), in0=sv(2), in1=sv(4))
            v.tensor_copy(out=statb[0:1, 0:R], in_=sv(4))
            v.tensor_copy(out=statb[0:1, R:2 * R], in_=sv(3))
            mk.set(f"l6_statb{li}", v.n)

        @block.gpsimd
        def _(g):
            for i in range(KT):
                g.dma_start(out=xbf[:, i * R:(i + 1) * R],
                            in_=x1_i[i * 128:(i + 1) * 128, :]).then_inc(s_cp, 16)
            g.dma_start(out=lnwbs[:], in_=lnwb).then_inc(s_ld, 16)
            g.dma_start(out=fc1bs[:], in_=fc1b).then_inc(s_ld, 16)
            g.dma_start(out=fc2bs[:], in_=fc2b).then_inc(s_ld, 16)
            g.dma_start(out=wdtb[:].rearrange("p (t c) -> p t c", c=64),
                        in_=wdt_t.rearrange("(t p) c -> p t c", p=128)).then_inc(s_ld, 16)
            for si in range(WCT):
                b = si % 3
                if si >= 3:
                    g.wait_ge(s_tch, mk.get(f"l6_pj{si - 3}"))
                g.dma_start(out=wbuf[:, b * D:(b + 1) * D].rearrange("p (t c) -> p t c", c=128),
                            in_=win_t[si].rearrange("(t p) c -> p t c", p=128)).then_inc(s_w[b], 16)
            for rp in range(2):
                for sb in range(2):
                    for nt in range(32):
                        si = sb * 32 + nt
                        gi = WCT + rp * 64 + si
                        b = gi % 3
                        if gi - 3 < WCT:
                            g.wait_ge(s_tch, mk.get(f"l6_pj{gi - 3}"))
                        else:
                            g.wait_ge(s_tch, mk.get(f"l6_w3f{gi - 3}"))
                        g.dma_start(out=wbuf[:, b * D:(b + 1) * D].rearrange("p (t c) -> p t c", c=128),
                                    in_=fc1_t[si].rearrange("(t p) c -> p t c", p=128)).then_inc(s_w[b], 16)
                    for hc in range(KT):
                        di = (rp * 2 + sb) * KT + hc
                        b = di % 2
                        if di >= 2:
                            g.wait_ge(s_tch, mk.get(f"l6_dn{di - 2}"))
                        g.dma_start(out=w2buf[:, b * 4096:(b + 1) * 4096].rearrange("p (t c) -> p t c", c=128),
                                    in_=fc2_t[hc][sb * 4096:(sb + 1) * 4096, :].rearrange("(t p) c -> p t c", p=128)
                                    ).then_inc(s_w2[b], 16)

        @block.scalar
        def _(s_):
            s = CEng(s_, s_sch)
            s.wait_ge(s_cp, 16 * KT)
            for li, srcb in ((0, xbf), (1, xn1)):
                if li == 1:
                    s.wait_ge(s_vch, mk.get("l6_xn_all"))
                for i in range(KT):
                    if i >= 2:
                        s.wait_ge(s_tch, mk.get(f"l6_sqc{li}_{i - 2}"))
                    s.activation(out=sq[:, (i % 2) * R:((i % 2) + 1) * R],
                                 in_=srcb[:, i * R:(i + 1) * R], func=AF.Square)
                    mk.set(f"l6_sq{li}_{i}", s.n)
                s.wait_ge(s_vch, mk.get(f"l6_rcp{li}"))
                s.activation(out=stat[0:1, 4 * R:5 * R], in_=stat[0:1, 3 * R:4 * R], func=AF.Sqrt)
                mk.set(f"l6_sqrt{li}", s.n)
                s.wait_ge(s_tch, mk.get(f"l6_bc{li}"))
                if li == 1:
                    s.wait_ge(s_vch, mk.get("l6_xn_all"))
                s.activation(out=rstd_b[:], in_=pC[:, 0:R], func=AF.Identity)
                s.activation(out=mrs_b[:], in_=pD[:, 0:R], func=AF.Identity)
                mk.set(f"l6_be{li}", s.n)
            for si in range(WCT + 1):
                s.wait_ge(s_tch, mk.get(f"l6_pj{si}"))
                oslot = si % 2
                if si >= 2:
                    s.wait_ge(s_oz[oslot], 16 * (si // 2))
                s.activation(out=ost[:, oslot * R:(oslot + 1) * R],
                             in_=PS[si % 4][:, 0:R], func=AF.Identity)
                mk.set(f"l6_ev{si}", s.n)
            for rp in range(2):
                for sb in range(2):
                    for nt in range(32):
                        nct = sb * 32 + nt
                        gi3 = rp * 64 + nct
                        sl = gi3 % 2
                        s.wait_ge(s_tch, mk.get(f"l6_f1{rp}_{nct}"))
                        if gi3 >= 2:
                            pg = gi3 - 2
                            s.wait_ge(s_vch, mk.get(f"l6_it{pg // 64}_{pg % 64}"))
                        s.activation(out=g0[:, sl * RP:(sl + 1) * RP], in_=PS[2 + sl][:, 0:RP],
                                     func=AF.Identity, bias=fc1bs[:, nct:nct + 1])
                        mk.set(f"l6_f1e{rp}_{nct}", s.n)
                        s.wait_ge(s_vch, mk.get(f"l6_v2{rp}_{nct}"))
                        s.activation(out=tv[:, sl * RP:(sl + 1) * RP], in_=tv[:, sl * RP:(sl + 1) * RP],
                                     func=AF.Tanh, scale=0.7978845608028654)
                        mk.set(f"l6_th{rp}_{nct}", s.n)
                    for hc in range(KT):
                        di = (rp * 2 + sb) * KT + hc
                        sl = di % 2
                        s.wait_ge(s_tch, mk.get(f"l6_dn{di}"))
                        if sb == 0:
                            s.activation(out=acc[:, hc * RP:(hc + 1) * RP], in_=PS[sl][:, 0:RP],
                                         func=AF.Identity)
                        mk.set(f"l6_as{di}", s.n)

        @block.vector
        def _(v_):
            v = CEng(v_, s_vch)
            v.memset(ones_r[:], 1.0)
            v.memset(ones_c[:], 1.0)
            mk.set("l6_ones", v.n)
            ln_stats_vector(v, 0, xbf)
            v.wait_ge(s_sch, mk.get("l6_be0"))
            for i in range(KT):
                t0 = xn1[:, i * R:(i + 1) * R]
                v.tensor_mul(out=t0, in0=xbf[:, i * R:(i + 1) * R], in1=rstd_b[:])
                v.tensor_sub(out=t0, in0=t0, in1=mrs_b[:])
                mk.set(f"l6_xn{i}", v.n)
            mk.set("l6_xn_all", v.n)
            ln_stats_vector(v, 1, xn1)
            v.wait_ge(s_sch, mk.get("l6_be1"))
            v.wait_ge(s_ld, 16 * 4)
            for i in range(KT):
                t0 = hs[:, i * R:(i + 1) * R]
                v.tensor_mul(out=t0, in0=xn1[:, i * R:(i + 1) * R], in1=rstd_b[:])
                v.tensor_sub(out=t0, in0=t0, in1=mrs_b[:])
                v.tensor_scalar(out=t0, in0=t0, scalar1=lnwbs[:, i:i + 1],
                                scalar2=lnwbs[:, KT + i:KT + i + 1], op0=ALU.mult, op1=ALU.add)
                mk.set(f"l6_hs{i}", v.n)
            for rp in range(2):
                for sb in range(2):
                    for nt in range(32):
                        nct = sb * 32 + nt
                        sl = (rp * 64 + nct) % 2
                        S = lambda buf: buf[:, sl * RP:(sl + 1) * RP]
                        v.wait_ge(s_sch, mk.get(f"l6_f1e{rp}_{nct}"))
                        v.tensor_mul(out=S(tu), in0=S(g0), in1=S(g0))
                        v.tensor_scalar(out=S(tv), in0=S(tu), scalar1=0.044715, scalar2=1.0,
                                        op0=ALU.mult, op1=ALU.add)
                        v.tensor_mul(out=S(tv), in0=S(tv), in1=S(g0))
                        mk.set(f"l6_v2{rp}_{nct}", v.n)
                        v.wait_ge(s_sch, mk.get(f"l6_th{rp}_{nct}"))
                        v.tensor_scalar(out=S(tu), in0=S(tv), scalar1=1.0, scalar2=0.5,
                                        op0=ALU.add, op1=ALU.mult)
                        if rp + sb > 0:
                            prev_last_di = (rp * 2 + sb - 1 + 1) * KT - 1
                            v.wait_ge(s_tch, mk.get(f"l6_dn{prev_last_di}"))
                        v.tensor_mul(out=itb[:, nt * RP:(nt + 1) * RP], in0=S(tu), in1=S(g0))
                        mk.set(f"l6_it{rp}_{nct}", v.n)
                    for hc in range(KT):
                        di = (rp * 2 + sb) * KT + hc
                        sl = di % 2
                        if sb > 0:
                            v.wait_ge(s_tch, mk.get(f"l6_dn{di}"))
                            v.tensor_add(out=acc[:, hc * RP:(hc + 1) * RP],
                                         in0=acc[:, hc * RP:(hc + 1) * RP], in1=PS[sl][:, 0:RP])
                        else:
                            v.wait_ge(s_sch, mk.get(f"l6_as{di}"))
                        mk.set(f"l6_av{di}", v.n)
                for hc in range(KT):
                    xi = rp * KT + hc
                    sl = xi % 2
                    S2 = lambda buf: buf[:, sl * RP:(sl + 1) * RP]
                    if xi >= 2:
                        v.wait_ge(s_oa[sl], 16 * (xi // 2))
                    v.tensor_add(out=S2(ost2), in0=acc[:, hc * RP:(hc + 1) * RP],
                                 in1=xn1[:, hc * R + rp * RP:hc * R + (rp + 1) * RP])
                    v.tensor_scalar(out=S2(ost2), in0=S2(ost2), scalar1=fc2bs[:, hc:hc + 1],
                                    scalar2=None, op0=ALU.add)
                    mk.set(f"l6_x{rp}_{hc}", v.n)

        @block.tensor
        def _(t_):
            t = CEng(t_, s_tch, auto=False)
            t.wait_ge(s_vch, mk.get("l6_ones"))
            t.wait_ge(s_cp, 16 * KT)
            for li, srcb in ((0, xbf), (1, xn1)):
                if li == 1:
                    t.wait_ge(s_vch, mk.get("l6_xn_all"))
                for i in range(KT):
                    t.wait_ge(s_sch, mk.get(f"l6_sq{li}_{i}"))
                    for ch in range(CH):
                        t.matmul(pA[0:1, ch * 512:(ch + 1) * 512], ones_c[:],
                                 srcb[:, i * R + ch * 512:i * R + (ch + 1) * 512],
                                 start=(i == 0), stop=(i == KT - 1))
                        ins = t.matmul(pB[0:1, ch * 512:(ch + 1) * 512], ones_c[:],
                                       sq[:, (i % 2) * R + ch * 512:(i % 2) * R + (ch + 1) * 512],
                                       start=(i == 0), stop=(i == KT - 1))
                    t.mark(ins)
                    mk.set(f"l6_sqc{li}_{i}", t.n)
                mk.set(f"l6_ln{li}_mm", t.n)
                t.wait_ge(s_vch, mk.get(f"l6_statb{li}"))
                for ch in range(CH):
                    t.matmul(pC[:, ch * 512:(ch + 1) * 512], ones_r[:],
                             statb[0:1, ch * 512:(ch + 1) * 512], start=True, stop=True)
                    ins = t.matmul(pD[:, ch * 512:(ch + 1) * 512], ones_r[:],
                                   statb[0:1, R + ch * 512:R + (ch + 1) * 512], start=True, stop=True)
                t.mark(ins)
                mk.set(f"l6_bc{li}", t.n)
            t.wait_ge(s_vch, mk.get(f"l6_hs{KT - 1}"))
            t.wait_ge(s_ld, 16 * 4)
            for si in range(WCT + 1):
                pdst = PS[si % 4]
                if si < WCT:
                    b = si % 3
                    t.wait_ge(s_w[b], 16 * (si // 3 + 1))
                if si >= 4:
                    t.wait_ge(s_sch, mk.get(f"l6_ev{si - 4}"))
                for k in range(KT):
                    if si < WCT:
                        wt = wbuf[:, b * D + k * 128:b * D + (k + 1) * 128]
                    else:
                        wt = wdtb[:, k * 64:(k + 1) * 64]
                    for ch in range(CH):
                        ins = t.matmul(pdst[0:64 if si == WCT else 128, ch * 512:(ch + 1) * 512],
                                       wt, hs[:, k * R + ch * 512:k * R + (ch + 1) * 512],
                                       start=(k == 0), stop=(k == KT - 1))
                t.mark(ins)
                mk.set(f"l6_pj{si}", t.n)
            for rp in range(2):
                for sb in range(2):
                    for nt in range(32):
                        nct = sb * 32 + nt
                        gi = WCT + rp * 64 + nct
                        gi3 = rp * 64 + nct
                        b = gi % 3
                        sl = gi3 % 2
                        t.wait_ge(s_w[b], 16 * (gi // 3 + 1))
                        if gi3 == 0:
                            t.wait_ge(s_sch, mk.get("l6_ev78"))
                        elif gi3 == 1:
                            t.wait_ge(s_sch, mk.get("l6_ev79"))
                        else:
                            pg = gi3 - 2
                            t.wait_ge(s_sch, mk.get(f"l6_f1e{pg // 64}_{pg % 64}"))
                        for k in range(KT):
                            ins = t.matmul(PS[2 + sl][:, 0:RP], wbuf[:, b * D + k * 128:b * D + (k + 1) * 128],
                                           hs[:, k * R + rp * RP:k * R + (rp + 1) * RP],
                                           start=(k == 0), stop=(k == KT - 1))
                        t.mark(ins)
                        mk.set(f"l6_f1{rp}_{nct}", t.n)
                        mk.set(f"l6_w3f{gi}", t.n)
                    for hc in range(KT):
                        di = (rp * 2 + sb) * KT + hc
                        b = di % 2
                        sl = di % 2
                        t.wait_ge(s_w2[b], 16 * (di // 2 + 1))
                        t.wait_ge(s_vch, mk.get(f"l6_it{rp}_{sb * 32 + 31}"))
                        if di == 0:
                            t.wait_ge(s_sch, mk.get("l6_ev80"))
                        elif di == 1:
                            t.wait_ge(s_sch, mk.get("l6_ev77"))
                        else:
                            t.wait_ge(s_vch, mk.get(f"l6_av{di - 2}"))
                            t.wait_ge(s_sch, mk.get(f"l6_as{di - 2}"))
                        for k in range(32):
                            ins = t.matmul(PS[sl][:, 0:RP], w2buf[:, b * 4096 + k * 128:b * 4096 + (k + 1) * 128],
                                           itb[:, k * RP:(k + 1) * RP],
                                           start=(k == 0), stop=(k == 31))
                        t.mark(ins)
                        mk.set(f"l6_dn{di}", t.n)

        @block.sync
        def _(sy):
            for i in range(KT):
                sy.wait_ge(s_vch, mk.get(f"l6_xn{i}"))
                sy.dma_start(out=xn1_o[i * 128:(i + 1) * 128, :],
                             in_=xn1[:, i * R:(i + 1) * R]).then_inc(s_o1, 16)
            for si in range(WCT + 1):
                sy.wait_ge(s_sch, mk.get(f"l6_ev{si}"))
                oslot = si % 2
                if si < WCT:
                    sy.dma_start(out=zx_o[si], in_=ost[:, oslot * R:(oslot + 1) * R]).then_inc(s_oz[oslot], 16)
                else:
                    sy.dma_start(out=dt_o, in_=ost[0:64, oslot * R:(oslot + 1) * R]).then_inc(s_oz[oslot], 16)
            for rp in range(2):
                for hc in range(KT):
                    xi = rp * KT + hc
                    sl = xi % 2
                    sy.wait_ge(s_vch, mk.get(f"l6_x{rp}_{hc}"))
                    sy.dma_start(out=acc1_o[hc * 128:(hc + 1) * 128, rp * RP:(rp + 1) * RP],
                                 in_=ost2[:, sl * RP:(sl + 1) * RP]).then_inc(s_oa[sl], 16)
            sy.wait_ge(s_o1, 16 * KT)
            for j in range(2):
                sy.wait_ge(s_oz[j], 16 * ((WCT + 1 + 1 - j) // 2))
            for j in range(2):
                sy.wait_ge(s_oa[j], 16 * KT)

    return nc


def host_prep_l6(inputs, x1_shards):
    win = np.asarray(inputs["L1_attn_W_in"], np.float32)
    win_t = tile_w(np.ascontiguousarray(win[:, :10240]))
    wdt = np.ascontiguousarray(win[:, 10240:])
    lnw = np.asarray(inputs["L1_attn_ln_w"], np.float32).reshape(KT, 128).T
    lnb = np.asarray(inputs["L1_attn_ln_b"], np.float32).reshape(KT, 128).T
    lnwb = np.ascontiguousarray(np.concatenate([lnw, lnb], axis=1))
    fc1_t = tile_w(np.ascontiguousarray(np.asarray(inputs["L1_attn_fc1_w"], np.float32)))
    fc1b = np.ascontiguousarray(np.asarray(inputs["L1_attn_fc1_b"], np.float32).reshape(64, 128).T)
    fc2_t = tile_w(np.ascontiguousarray(np.asarray(inputs["L1_attn_fc2_w"], np.float32)))
    fc2b = np.ascontiguousarray(np.asarray(inputs["L1_attn_fc2_b"], np.float32).reshape(KT, 128).T)
    maps = []
    for i in range(NC):
        maps.append({
            "x1_i": np.ascontiguousarray(x1_shards[i]),
            "win_t": win_t, "wdt_t": wdt, "lnwb": lnwb,
            "fc1_t": fc1_t, "fc1b": fc1b, "fc2_t": fc2_t, "fc2b": fc2b,
        })
    return maps

NCH = 8
NH = 64


def build_mm_once(mk):
    nc = bass.Bass("TRN2", target_bir_lowering=False, debug=False)
    bh_i = nc.dram_tensor("bh_i", [NCH, 64, NH, 128], BF16, kind="ExternalInput").ap()
    ch_i = nc.dram_tensor("ch_i", [NCH, 64, NH, 128], BF16, kind="ExternalInput").ap()
    xt_i = nc.dram_tensor("xt_i", [NCH, 128, NH, 64], BF16, kind="ExternalInput").ap()
    bt_i = nc.dram_tensor("bt_i", [NCH, 128, NH, 64], BF16, kind="ExternalInput").ap()
    dt_i = nc.dram_tensor("dt_i", [R, NH], BF16, kind="ExternalInput").ap()
    ab_i = nc.dram_tensor("ab_i", [1, 512], F32, kind="ExternalInput").ap()
    db_i = nc.dram_tensor("db_i", [1, 512], F32, kind="ExternalInput").ap()
    tri_i = nc.dram_tensor("tri_i", [128, 4 * 128], BF16, kind="ExternalInput").ap()
    id_i = nc.dram_tensor("id_i", [128, 128], BF16, kind="ExternalInput").ap()
    y_o = nc.dram_tensor("y_o", [NCH, 64, NH, 128], BF16, kind="ExternalOutput").ap()
    st_o = nc.dram_tensor("st_o", [64, NH * 64], F32, kind="ExternalOutput").ap()
    rc_o = nc.dram_tensor("rc_o", [NCH, 64, NH, 128], BF16, kind="ExternalOutput").ap()
    cs_o = nc.dram_tensor("cs_o", [1, 512], F32, kind="ExternalOutput").ap()

    with ExitStack() as ctx:
        e = ctx.enter_context
        bh = e(nc.sbuf_tensor("bh", [64, 2 * NH * 128], BF16))
        chs = e(nc.sbuf_tensor("chs", [64, 2 * NH * 128], BF16))
        xtk = e(nc.sbuf_tensor("xtk", [128, 2 * NH * 64], BF16))
        btk = e(nc.sbuf_tensor("btk", [128, 2 * NH * 64], BF16))
        dts = e(nc.sbuf_tensor("dts", [128, 512], F32))
        dtt = e(nc.sbuf_tensor("dtt", [128, 512], BF16))
        lbf = e(nc.sbuf_tensor("lbf", [128, 512], BF16))
        lf32 = e(nc.sbuf_tensor("lf32", [128, 512], F32))
        tmp = e(nc.sbuf_tensor("tmp", [128, 2 * 512], F32))
        ab_r = e(nc.sbuf_tensor("ab_r", [1, 2 * 512], F32))
        abneg = e(nc.sbuf_tensor("abneg", [1, 2 * 512], BF16))
        abk = e(nc.sbuf_tensor("abk", [128, 2 * 512], BF16))
        tri = e(nc.sbuf_tensor("tri", [128, 4 * 128], BF16))
        idb = e(nc.sbuf_tensor("idb", [128, 128], BF16))
        rhsb = e(nc.sbuf_tensor("rhsb", [128, 2 * 128], BF16))
        expD = e(nc.sbuf_tensor("expD", [128, 2 * 256], F32))
        gsb = e(nc.sbuf_tensor("gsb", [128, 2 * 128], BF16))
        emb = e(nc.sbuf_tensor("emb", [128, 2 * 128], F32))
        gmb = e(nc.sbuf_tensor("gmb", [128, 2 * 128], BF16))
        rcb = e(nc.sbuf_tensor("rcb", [64, 2 * NH * 128], BF16))
        wbb = e(nc.sbuf_tensor("wbb", [128, 2 * 64], BF16))
        sins = e(nc.sbuf_tensor("sins", [64, NH * 64], F32))
        sinb = e(nc.sbuf_tensor("sinb", [64, 2 * 64], BF16))
        sisc = e(nc.sbuf_tensor("sisc", [64, 2 * 64], BF16))
        ybf = e(nc.sbuf_tensor("ybf", [64, 2 * NH * 128], BF16))
        csb = e(nc.sbuf_tensor("csb", [1, 512], F32))
        psD = e(nc.psum_tensor("psD", [128, 512], F32))
        psD2 = e(nc.psum_tensor("psD2", [128, 512], F32))
        psG = e(nc.psum_tensor("psG", [128, 512], F32))
        psG2 = e(nc.psum_tensor("psG2", [128, 512], F32))
        psY = e(nc.psum_tensor("psY", [128, 512], F32))
        psS = e(nc.psum_tensor("psS", [128, 512], F32))
        psL = e(nc.psum_tensor("psL", [128, 1024], F32))
        PDs = (psD, psD2)
        PGs = (psG, psG2)

        s_ld = e(nc.semaphore("s_ld"))
        s_lc = [e(nc.semaphore(f"s_lc{j}")) for j in range(2)]
        s_oy = [e(nc.semaphore(f"s_oy{j}")) for j in range(2)]
        s_or = [e(nc.semaphore(f"s_or{j}")) for j in range(2)]
        s_vch = e(nc.semaphore("s_vch"))
        s_sch = e(nc.semaphore("s_sch"))
        s_tch = e(nc.semaphore("s_tch"))
        block = e(nc.Block())

        @block.sync
        def _(sy):
            sy.dma_start(out=dtt[:].rearrange("p (c h) -> p c h", h=NH),
                         in_=dt_i.rearrange("(c p) h -> p c h", p=128)).then_inc(s_ld, 16)
            sy.dma_start(out=ab_r[0:1, 0:512], in_=ab_i).then_inc(s_ld, 16)
            sy.dma_start(out=ab_r[0:1, 512:1024], in_=db_i).then_inc(s_ld, 16)
            sy.dma_start(out=tri[:], in_=tri_i).then_inc(s_ld, 16)
            sy.dma_start(out=idb[:], in_=id_i).then_inc(s_ld, 16)

        @block.gpsimd
        def _(g):
            for c in range(NCH):
                sl = c % 2
                if c >= 2:
                    g.wait_ge(s_tch, mk.get(f"m_cend{c - 2}"))
                g.dma_start(out=bh[:, sl * NH * 128:(sl + 1) * NH * 128].rearrange("p (h t) -> p h t", t=128),
                            in_=bh_i[c]).then_inc(s_lc[sl], 16)
                g.dma_start(out=chs[:, sl * NH * 128:(sl + 1) * NH * 128].rearrange("p (h t) -> p h t", t=128),
                            in_=ch_i[c]).then_inc(s_lc[sl], 16)
                g.dma_start(out=xtk[:, sl * NH * 64:(sl + 1) * NH * 64].rearrange("p (h q) -> p h q", q=64),
                            in_=xt_i[c]).then_inc(s_lc[sl], 16)
                g.dma_start(out=btk[:, sl * NH * 64:(sl + 1) * NH * 64].rearrange("p (h q) -> p h q", q=64),
                            in_=bt_i[c]).then_inc(s_lc[sl], 16)

        @block.scalar
        def _(s_):
            s = CEng(s_, s_sch)
            s.wait_ge(s_ld, 16 * 5)
            s.activation(out=tmp[0:1, 0:512], in_=ab_r[0:1, 0:512], func=AF.Exp)
            s.activation(out=abneg[0:1, 0:512], in_=tmp[0:1, 0:512], func=AF.Identity, scale=-1.0)
            s.activation(out=abneg[0:1, 512:1024], in_=ab_r[0:1, 512:1024], func=AF.Identity)
            mk.set("m_abneg", s.n)
            s.wait_ge(s_tch, mk.get("m_abkmm"))
            s.activation(out=abk[:, 0:512], in_=psL[:, 0:512], func=AF.Identity)
            s.activation(out=abk[:, 512:1024], in_=psL[:, 512:1024], func=AF.Identity)
            mk.set("m_abk", s.n)
            s.wait_ge(s_vch, mk.get("m_T0"))
            s.activation(out=tmp[:, 512:1024], in_=tmp[:, 0:512], func=AF.Abs)
            s.activation(out=tmp[:, 512:1024], in_=tmp[:, 512:1024], func=AF.Exp, scale=-1.0)
            s.activation(out=tmp[:, 512:1024], in_=tmp[:, 512:1024], func=AF.Ln, bias=1.0)
            mk.set("m_spexp", s.n)
            for c in range(NCH):
                sl = c % 2
                for h in range(NH):
                    u = c * NH + h
                    us = u % 2
                    s.wait_ge(s_tch, mk.get(f"m_D{u}"))
                    if u >= 2:
                        s.wait_ge(s_vch, mk.get(f"m_si{u - 2}"))
                    s.activation(out=expD[:, us * 256:(us + 1) * 256], in_=PDs[us][:, 0:256],
                                 func=AF.Exp)
                    mk.set(f"m_eD{u}", s.n)
                    s.wait_ge(s_tch, mk.get(f"m_G{u}"))
                    if u >= 2:
                        s.wait_ge(s_vch, mk.get(f"m_gm{u - 2}"))
                    s.activation(out=gsb[:, us * 128:(us + 1) * 128], in_=PGs[us][:, 0:128],
                                 func=AF.Identity)
                    mk.set(f"m_eG{u}", s.n)
                    s.wait_ge(s_tch, mk.get(f"m_Y{u}"))
                    if c >= 2:
                        s.wait_ge(s_oy[sl], 16 * (c // 2))
                    s.activation(out=ybf[0:64, sl * NH * 128 + h * 128:sl * NH * 128 + (h + 1) * 128],
                                 in_=psY[0:64, 0:128], func=AF.Identity)
                    mk.set(f"m_eY{u}", s.n)
                    s.wait_ge(s_tch, mk.get(f"m_S{u}"))
                    s.activation(out=sins[:, h * 64:(h + 1) * 64], in_=psS[0:64, 0:64],
                                 func=AF.Identity)
                    mk.set(f"m_eS{u}", s.n)
            s.wait_ge(s_tch, mk.get("m_cs"))
            s.activation(out=csb[:], in_=psL[0:1, 0:512], func=AF.Identity)
            mk.set("m_ecs", s.n)

        @block.vector
        def _(v_):
            v = CEng(v_, s_vch)
            v.wait_ge(s_ld, 16 * 5)
            v.wait_ge(s_sch, mk.get("m_abk"))
            T0 = tmp[:, 0:512]
            T1 = tmp[:, 512:1024]
            v.tensor_add(out=T0, in0=dtt[:], in1=abk[:, 512:1024])
            mk.set("m_T0", v.n)
            v.wait_ge(s_sch, mk.get("m_spexp"))
            v.tensor_scalar(out=dts[:], in0=T0, scalar1=0.0, scalar2=None, op0=ALU.max)
            v.tensor_add(out=dts[:], in0=dts[:], in1=T1)
            v.tensor_mul(out=lf32[:], in0=dts[:], in1=abk[:, 0:512])
            v.tensor_copy(out=lbf[:], in_=lf32[:])
            mk.set("m_l", v.n)
            for c in range(NCH):
                sl = c % 2
                CO = sl * NH * 128
                XO = sl * NH * 64
                for h in range(NH):
                    u = c * NH + h
                    us = u % 2
                    lcol = lf32[:, c * 64 + h:c * 64 + h + 1]
                    dcol = dts[:, c * 64 + h:c * 64 + h + 1]
                    if u >= 2:
                        v.wait_ge(s_tch, mk.get(f"m_D{u - 2}"))
                    v.tensor_scalar(out=rhsb[:, us * 128:(us + 1) * 128], in0=tri[:, 256:384],
                                    scalar1=lcol, scalar2=None, op0=ALU.mult)
                    mk.set(f"m_rhs{u}", v.n)
                    v.wait_ge(s_sch, mk.get(f"m_eD{u}"))
                    v.tensor_scalar(out=emb[:, us * 128:(us + 1) * 128], in0=tri[:, 256:384],
                                    scalar1=dcol, scalar2=None, op0=ALU.mult)
                    v.tensor_mul(out=emb[:, us * 128:(us + 1) * 128],
                                 in0=emb[:, us * 128:(us + 1) * 128],
                                 in1=expD[:, us * 256:us * 256 + 128])
                    mk.set(f"m_em{u}", v.n)
                    v.wait_ge(s_sch, mk.get(f"m_eG{u}"))
                    if u >= 2:
                        v.wait_ge(s_tch, mk.get(f"m_Y{u - 2}"))
                    v.tensor_mul(out=gmb[:, us * 128:(us + 1) * 128],
                                 in0=gsb[:, us * 128:(us + 1) * 128],
                                 in1=emb[:, us * 128:(us + 1) * 128])
                    mk.set(f"m_gm{u}", v.n)
                    v.wait_ge(s_lc[sl], 16 * 4 * (c // 2 + 1))
                    if c >= 2 and h == 0:
                        v.wait_ge(s_or[sl], 16 * (c // 2))
                    v.tensor_mul(out=rcb[0:64, CO + h * 128:CO + (h + 1) * 128],
                                 in0=expD[0:64, us * 256 + 128:us * 256 + 256],
                                 in1=chs[0:64, CO + h * 128:CO + (h + 1) * 128])
                    mk.set(f"m_rc{u}", v.n)
                    if u >= 2:
                        v.wait_ge(s_tch, mk.get(f"m_S{u - 2}"))
                    v.tensor_scalar(out=wbb[:, us * 64:(us + 1) * 64],
                                    in0=btk[:, XO + h * 64:XO + (h + 1) * 64],
                                    scalar1=emb[:, us * 128 + 127:us * 128 + 128],
                                    scalar2=None, op0=ALU.mult)
                    mk.set(f"m_wb{u}", v.n)
                    if u >= 2:
                        v.wait_ge(s_tch, mk.get(f"m_Y{u - 2}"))
                        v.wait_ge(s_tch, mk.get(f"m_S{u - 2}"))
                    if c == 0:
                        v.memset(sinb[:, us * 64:(us + 1) * 64], 0.0)
                        v.memset(sisc[:, us * 64:(us + 1) * 64], 0.0)
                    else:
                        v.wait_ge(s_sch, mk.get(f"m_eS{(c - 1) * NH + h}"))
                        v.tensor_copy(out=sinb[:, us * 64:(us + 1) * 64],
                                      in_=sins[:, h * 64:(h + 1) * 64])
                        v.tensor_scalar(out=sisc[:, us * 64:(us + 1) * 64],
                                        in0=sinb[:, us * 64:(us + 1) * 64],
                                        scalar1=expD[0:64, us * 256 + 255:us * 256 + 256],
                                        scalar2=None, op0=ALU.mult)
                    mk.set(f"m_si{u}", v.n)

        @block.tensor
        def _(t_):
            t = CEng(t_, s_tch, auto=False)
            t.wait_ge(s_ld, 16 * 5)
            t.wait_ge(s_sch, mk.get("m_abneg"))
            t.matmul(psL[:, 0:512], tri[0:1, 384:512], abneg[0:1, 0:512], start=True, stop=True)
            ins = t.matmul(psL[:, 512:1024], tri[0:1, 384:512], abneg[0:1, 512:1024],
                           start=True, stop=True)
            t.mark(ins)
            mk.set("m_abkmm", t.n)
            t.wait_ge(s_vch, mk.get("m_l"))
            t.wait_ge(s_sch, mk.get("m_abk"))
            ins = t.matmul(psL[0:1, 0:512], tri[:, 384:385], lbf[:], start=True, stop=True)
            t.mark(ins)
            mk.set("m_cs", t.n)
            for c in range(NCH):
                sl = c % 2
                CO = sl * NH * 128
                XO = sl * NH * 64
                t.wait_ge(s_lc[sl], 16 * 4 * (c // 2 + 1))
                for h in range(NH):
                    u = c * NH + h
                    us = u % 2
                    t.wait_ge(s_vch, mk.get(f"m_rhs{u}"))
                    if u >= 2:
                        t.wait_ge(s_sch, mk.get(f"m_eD{u - 2}"))
                    t.matmul(PDs[us][:, 0:128], tri[:, 128:256], rhsb[:, us * 128:(us + 1) * 128],
                             start=True, stop=True)
                    ins = t.matmul(PDs[us][:, 128:256], tri[:, 384:512],
                                   rhsb[:, us * 128:(us + 1) * 128], start=True, stop=True)
                    t.mark(ins)
                    mk.set(f"m_D{u}", t.n)
                    if u >= 2:
                        t.wait_ge(s_sch, mk.get(f"m_eG{u - 2}"))
                    ins = t.matmul(PGs[us][:, 0:128], bh[0:64, CO + h * 128:CO + (h + 1) * 128],
                                   chs[0:64, CO + h * 128:CO + (h + 1) * 128], start=True, stop=True)
                    t.mark(ins)
                    mk.set(f"m_G{u}", t.n)
                    t.wait_ge(s_vch, mk.get(f"m_gm{u}"))
                    t.wait_ge(s_vch, mk.get(f"m_si{u}"))
                    t.wait_ge(s_vch, mk.get(f"m_rc{u}"))
                    if u >= 1:
                        t.wait_ge(s_sch, mk.get(f"m_eY{u - 1}"))
                    t.matmul(psY[0:64, 0:128], xtk[:, XO + h * 64:XO + (h + 1) * 64],
                             gmb[:, us * 128:(us + 1) * 128], start=True, stop=False)
                    ins = t.matmul(psY[0:64, 0:128], sinb[:, us * 64:(us + 1) * 64],
                                   rcb[0:64, CO + h * 128:CO + (h + 1) * 128],
                                   start=False, stop=True)
                    t.mark(ins)
                    mk.set(f"m_Y{u}", t.n)
                    t.wait_ge(s_vch, mk.get(f"m_wb{u}"))
                    if u >= 1:
                        t.wait_ge(s_sch, mk.get(f"m_eS{u - 1}"))
                    t.matmul(psS[0:64, 0:64], wbb[:, us * 64:(us + 1) * 64],
                             xtk[:, XO + h * 64:XO + (h + 1) * 64], start=True, stop=False)
                    ins = t.matmul(psS[0:64, 0:64], idb[0:64, 0:64],
                                   sisc[:, us * 64:(us + 1) * 64], start=False, stop=True)
                    t.mark(ins)
                    mk.set(f"m_S{u}", t.n)
                mk.set(f"m_cend{c}", t.n)

        @block.sync
        def _(sy):
            for c in range(NCH):
                sl = c % 2
                sy.wait_ge(s_sch, mk.get(f"m_eY{c * NH + NH - 1}"))
                sy.dma_start(out=y_o[c].rearrange("p h t -> p (h t)"),
                             in_=ybf[0:64, sl * NH * 128:(sl + 1) * NH * 128]).then_inc(s_oy[sl], 16)
                sy.wait_ge(s_vch, mk.get(f"m_rc{c * NH + NH - 1}"))
                sy.dma_start(out=rc_o[c].rearrange("p h t -> p (h t)"),
                             in_=rcb[0:64, sl * NH * 128:(sl + 1) * NH * 128]).then_inc(s_or[sl], 16)
            sy.wait_ge(s_sch, mk.get(f"m_eS{NCH * NH - 1}"))
            sy.dma_start(out=st_o, in_=sins[:]).then_inc(s_oy[0], 16)
            sy.wait_ge(s_sch, mk.get("m_ecs"))
            sy.dma_start(out=cs_o, in_=csb[:]).then_inc(s_oy[0], 16)
            sy.wait_ge(s_oy[0], 16 * (NCH // 2 + 2))
            sy.wait_ge(s_oy[1], 16 * (NCH // 2))
            for j in range(2):
                sy.wait_ge(s_or[j], 16 * (NCH // 2))
    return nc


def mm_consts():
    t_incl = np.tril(np.ones((128, 128), np.float32)).T
    strict = (np.triu(np.ones((128, 128), np.float32), 1) != 0).astype(np.float32)
    up = np.zeros((128, 128), np.float32)
    rr, ss = np.meshgrid(np.arange(128), np.arange(128), indexing="ij")
    up[rr > ss] = 1.0
    tri01 = (ss.T >= rr.T).astype(np.float32).T
    tri01 = (rr <= ss).astype(np.float32)
    ones = np.ones((128, 128), np.float32)
    tri = np.concatenate([t_incl, up, tri01, ones], axis=1).astype(BF)
    return np.ascontiguousarray(tri)


def host_prep_mm(inputs, l6_results):
    tri = mm_consts()
    idm = np.eye(128, dtype=BF)
    A8 = np.tile(np.asarray(inputs["L1_attn_A_log"], np.float32), 8)[None, :]
    db8 = np.tile(np.asarray(inputs["L1_attn_dt_bias"], np.float32), 8)[None, :]
    maps = []
    for i in range(NC):
        zx = np.asarray(l6_results[i]["zx_o"])
        full = zx.reshape(80 * 128, R)
        xbT = full[4096:5120]
        BT = full[5120:6144]
        CT = full[6144:10240]
        dtT = np.asarray(l6_results[i]["dt_o"])
        Ch = CT.reshape(NH, 64, R)
        ch = np.ascontiguousarray(
            Ch.reshape(NH, 64, NCH, 128).transpose(2, 1, 0, 3))
        Bg = BT.reshape(16, 64, R)
        Bh = np.repeat(Bg, 4, axis=0)
        bh = np.ascontiguousarray(
            Bh.reshape(NH, 64, NCH, 128).transpose(2, 1, 0, 3))
        xg = xbT.reshape(16, 64, R)
        xh = np.repeat(xg, 4, axis=0)
        xt = np.ascontiguousarray(
            xh.reshape(NH, 64, NCH, 128).transpose(2, 3, 0, 1))
        bt = np.ascontiguousarray(
            Bh.reshape(NH, 64, NCH, 128).transpose(2, 3, 0, 1))
        maps.append({
            "bh_i": bh, "ch_i": ch, "xt_i": xt, "bt_i": bt,
            "dt_i": np.ascontiguousarray(dtT.T), "ab_i": A8, "db_i": db8,
            "tri_i": tri, "id_i": idm,
        })
    return maps


def build_cmb_once(mk):
    nc = bass.Bass("TRN2", target_bir_lowering=False, debug=False)
    sta_i = nc.dram_tensor("sta_i", [8, 64, 4096], F32, kind="ExternalInput").ap()
    csa_i = nc.dram_tensor("csa_i", [64, 64], F32, kind="ExternalInput").ap()
    owncs_i = nc.dram_tensor("owncs_i", [64, 8], F32, kind="ExternalInput").ap()
    sel_i = nc.dram_tensor("sel_i", [64, 8], F32, kind="ExternalInput").ap()
    rc_i = nc.dram_tensor("rc_i", [NCH, 64, NH * 128], BF16, kind="ExternalInput").ap()
    y_i = nc.dram_tensor("y_i", [NCH, 64, NH * 128], BF16, kind="ExternalInput").ap()
    z_i = nc.dram_tensor("z_i", [NCH, 64, NH * 128], BF16, kind="ExternalInput").ap()
    xh_i = nc.dram_tensor("xh_i", [NCH, 64, NH * 128], BF16, kind="ExternalInput").ap()
    dch_i = nc.dram_tensor("dch_i", [64, NH], F32, kind="ExternalInput").ap()
    nw_i = nc.dram_tensor("nw_i", [128, 32], F32, kind="ExternalInput").ap()
    yn_o = nc.dram_tensor("yn_o", [4096, R], BF16, kind="ExternalOutput").ap()
    snc = nc.dram_tensor("snc", [NCH, 64, 4096], BF16).ap()
    ygd = nc.dram_tensor("ygd", [32, 128, R], BF16).ap()
    NT32 = 32

    with ExitStack() as ctx:
        e = ctx.enter_context
        sta = e(nc.sbuf_tensor("sta", [64, 4096], F32))
        csa = e(nc.sbuf_tensor("csa", [64, 64], F32))
        owncs = e(nc.sbuf_tensor("owncs", [64, 8], F32))
        sels = e(nc.sbuf_tensor("sels", [64, 8], F32))
        tot = e(nc.sbuf_tensor("tot", [64, 2 * 8], F32))
        qe = e(nc.sbuf_tensor("qe", [64, 2 * 8], F32))
        Pb = e(nc.sbuf_tensor("Pb", [64, 4096], F32))
        Pacc = e(nc.sbuf_tensor("Pacc", [64, 4096], F32))
        sincb = e(nc.sbuf_tensor("sincb", [64, 4096], BF16))
        sinct = e(nc.sbuf_tensor("sinct", [64, 4096], BF16))
        rcs = e(nc.sbuf_tensor("rcs", [64, NH * 128], BF16))
        ys = e(nc.sbuf_tensor("ys", [64, NH * 128], BF16))
        zs = e(nc.sbuf_tensor("zs", [64, NH * 128], BF16))
        xhs = e(nc.sbuf_tensor("xhs", [64, NH * 128], BF16))
        ygc = e(nc.sbuf_tensor("ygc", [64, NH * 128], BF16))
        sqc = e(nc.sbuf_tensor("sqc", [64, NH * 128], BF16))
        evb = e(nc.sbuf_tensor("evb", [64, 2 * 128], F32))
        dchs = e(nc.sbuf_tensor("dchs", [64, NH], F32))
        nws = e(nc.sbuf_tensor("nws", [128, 32], F32))
        stat = e(nc.sbuf_tensor("stat", [1, 5 * R], F32))
        statb = e(nc.sbuf_tensor("statb", [1, R], BF16))
        rstd_b = e(nc.sbuf_tensor("rstd_b", [128, R], BF16))
        ones_r = e(nc.sbuf_tensor("ones_r", [1, 128], BF16))
        ones_c = e(nc.sbuf_tensor("ones_c", [128, 1], BF16))
        ynt = e(nc.sbuf_tensor("ynt", [128, 2 * R], BF16))
        ost = e(nc.sbuf_tensor("ost", [128, 2 * R], BF16))
        psC = e(nc.psum_tensor("psC", [128, 512], F32))
        psC2 = e(nc.psum_tensor("psC2", [128, 512], F32))
        pS1 = e(nc.psum_tensor("pS1", [128, 1024], F32))
        pC = e(nc.psum_tensor("pC", [128, 1024], F32))
        PCs = (psC, psC2)

        s_ld = e(nc.semaphore("s_ld"))
        s_lt = e(nc.semaphore("s_lt"))
        s_sb = e(nc.semaphore("s_sb"))
        s_sr = e(nc.semaphore("s_sr"))
        s_yg = e(nc.semaphore("s_yg"))
        s_yl = [e(nc.semaphore(f"s_yl{j}")) for j in range(2)]
        s_oo = [e(nc.semaphore(f"s_oo{j}")) for j in range(2)]
        s_vch = e(nc.semaphore("s_vch"))
        s_sch = e(nc.semaphore("s_sch"))
        s_tch = e(nc.semaphore("s_tch"))
        block = e(nc.Block())

        @block.sync
        def _(sy):
            sy.dma_start(out=csa[:], in_=csa_i).then_inc(s_ld, 16)
            sy.dma_start(out=owncs[:], in_=owncs_i).then_inc(s_ld, 16)
            sy.dma_start(out=sels[:], in_=sel_i).then_inc(s_ld, 16)
            sy.dma_start(out=dchs[:], in_=dch_i).then_inc(s_ld, 16)
            sy.dma_start(out=nws[:], in_=nw_i).then_inc(s_ld, 16)
            for k in range(7):
                if k >= 1:
                    sy.wait_ge(s_vch, mk.get(f"c_Pk{k - 1}"))
                sy.dma_start(out=sta[:], in_=sta_i[k]).then_inc(s_lt, 16)
            for c in range(NCH):
                sy.wait_ge(s_vch, mk.get(f"c_snc{c}"))
                sy.dma_start(out=snc[c].rearrange("h (n p) -> h n p", p=64),
                             in_=sincb[:].rearrange("h (n p) -> h n p", p=64)).then_inc(s_sb, 16)

        @block.gpsimd
        def _(g):
            for c in range(NCH):
                if c >= 1:
                    g.wait_ge(s_vch, mk.get(f"c_done{c - 1}"))
                g.wait_ge(s_sb, 16 * (c + 1))
                g.dma_start(out=sinct[:].rearrange("n (h p) -> n h p", p=64),
                            in_=snc[c].rearrange("h (n p) -> n h p", p=64)).then_inc(s_sr, 16)
                g.dma_start(out=rcs[:], in_=rc_i[c]).then_inc(s_sr, 16)
                g.dma_start(out=ys[:], in_=y_i[c]).then_inc(s_sr, 16)
                g.dma_start(out=zs[:], in_=z_i[c]).then_inc(s_sr, 16)
                g.dma_start(out=xhs[:], in_=xh_i[c]).then_inc(s_sr, 16)
                g.wait_ge(s_vch, mk.get(f"c_gate{c}"))
                g.dma_start(out=ygd.rearrange("i (half p) r -> i half p r", p=64)
                            .rearrange("i half p (cc tt) -> i half p cc tt", tt=128)[:, :, :, c, :]
                            .rearrange("i half p tt -> p (i half) tt"),
                            in_=ygc[:].rearrange("p (t2 tt) -> p t2 tt", tt=128)
                            ).then_inc(s_yg, 16)
            for i in range(NT32):
                sl = i % 2
                g.wait_ge(s_yg, 16 * NCH)
                if i >= 2:
                    g.wait_ge(s_vch, mk.get(f"c_yn{i - 2}"))
                g.dma_start(out=ynt[:, sl * R:(sl + 1) * R], in_=ygd[i]).then_inc(s_yl[sl], 16)

        @block.scalar
        def _(s_):
            s = CEng(s_, s_sch)
            s.wait_ge(s_ld, 16 * 5)
            s.wait_ge(s_vch, mk.get("c_tot"))
            s.activation(out=tot[:, 8:16], in_=tot[:, 0:8], func=AF.Exp)
            mk.set("c_E", s.n)
            s.wait_ge(s_vch, mk.get("c_q"))
            s.activation(out=qe[:, 8:16], in_=qe[:, 0:8], func=AF.Exp)
            mk.set("c_qe", s.n)
            for c in range(NCH):
                s.wait_ge(s_sr, 16 * 5 * (c + 1))
                s.activation(out=zs[:], in_=zs[:], func=AF.Silu)
                mk.set(f"c_sz{c}", s.n)
                for h in range(NH):
                    u = c * NH + h
                    us = u % 2
                    s.wait_ge(s_tch, mk.get(f"c_pc{u}"))
                    if u >= 2:
                        s.wait_ge(s_vch, mk.get(f"c_yf{u - 2}"))
                    s.activation(out=evb[:, us * 128:(us + 1) * 128], in_=PCs[us][0:64, 0:128],
                                 func=AF.Identity)
                    mk.set(f"c_ev{u}", s.n)
                s.wait_ge(s_vch, mk.get(f"c_gate{c}"))
                s.activation(out=sqc[:], in_=ygc[:], func=AF.Square)
                mk.set(f"c_sq{c}", s.n)
            s.wait_ge(s_vch, mk.get("c_rcp"))
            s.activation(out=stat[0:1, 4 * R:5 * R], in_=stat[0:1, 3 * R:4 * R], func=AF.Sqrt)
            mk.set("c_sqrt", s.n)
            s.wait_ge(s_tch, mk.get("c_bc"))
            s.activation(out=rstd_b[:], in_=pC[:, 0:R], func=AF.Identity)
            mk.set("c_be", s.n)

        @block.vector
        def _(v_):
            v = CEng(v_, s_vch)
            v.memset(ones_r[:], 1.0)
            v.memset(ones_c[:], 1.0)
            mk.set("c_ones", v.n)
            v.wait_ge(s_ld, 16 * 5)
            for j in range(8):
                v.reduce_sum(out=tot[:, j:j + 1], in_=csa[:, j * 8:(j + 1) * 8], axis=AX.X)
            mk.set("c_tot", v.n)
            v.memset(qe[:, 0:1], 0.0)
            for c in range(1, 8):
                v.tensor_add(out=qe[:, c:c + 1], in0=qe[:, c - 1:c], in1=owncs[:, c - 1:c])
            mk.set("c_q", v.n)
            v.memset(Pb[:], 0.0)
            v.memset(Pacc[:], 0.0)
            v.wait_ge(s_sch, mk.get("c_E"))
            for k in range(7):
                v.wait_ge(s_lt, 16 * (k + 1))
                v.tensor_scalar(out=Pb[:], in0=Pb[:], scalar1=tot[:, 8 + k:9 + k],
                                scalar2=None, op0=ALU.mult)
                v.tensor_add(out=Pb[:], in0=Pb[:], in1=sta[:])
                v.scalar_tensor_tensor(out=Pacc[:], in0=Pb[:], scalar=sels[:, k:k + 1],
                                       in1=Pacc[:], op0=ALU.mult, op1=ALU.add)
                mk.set(f"c_Pk{k}", v.n)
            mk.set("c_P", v.n)
            v.wait_ge(s_sch, mk.get("c_qe"))
            for c in range(NCH):
                if c >= 1:
                    v.wait_ge(s_sb, 16 * c)
                v.tensor_scalar(out=sincb[:], in0=Pacc[:], scalar1=qe[:, 8 + c:9 + c],
                                scalar2=None, op0=ALU.mult)
                mk.set(f"c_snc{c}", v.n)
            for c in range(NCH):
                v.wait_ge(s_sr, 16 * 5 * (c + 1))
                for h in range(NH):
                    u = c * NH + h
                    us = u % 2
                    v.wait_ge(s_sch, mk.get(f"c_ev{u}"))
                    v.tensor_add(out=evb[:, us * 128:(us + 1) * 128],
                                 in0=evb[:, us * 128:(us + 1) * 128],
                                 in1=ys[0:64, h * 128:(h + 1) * 128])
                    v.scalar_tensor_tensor(out=ygc[0:64, h * 128:(h + 1) * 128],
                                           in0=xhs[0:64, h * 128:(h + 1) * 128],
                                           scalar=dchs[:, h:h + 1],
                                           in1=evb[0:64, us * 128:(us + 1) * 128],
                                           op0=ALU.mult, op1=ALU.add)
                    mk.set(f"c_yf{u}", v.n)
                v.wait_ge(s_sch, mk.get(f"c_sz{c}"))
                v.tensor_mul(out=ygc[:], in0=ygc[:], in1=zs[:])
                mk.set(f"c_gate{c}", v.n)
                v.wait_ge(s_tch, mk.get(f"c_st{c}"))
                v.wait_ge(s_yg, 16 * (c + 1))
                mk.set(f"c_done{c}", v.n)
            v.wait_ge(s_tch, mk.get("c_mm"))
            sv = lambda j: stat[0:1, j * R:(j + 1) * R]
            v.tensor_copy(out=sv(1), in_=pS1[0:1, 0:R])
            v.tensor_scalar(out=sv(4), in0=sv(1), scalar1=1.0 / 4096, scalar2=EPS,
                            op0=ALU.mult, op1=ALU.add)
            v.reciprocal(out=sv(3), in_=sv(4))
            mk.set("c_rcp", v.n)
            v.wait_ge(s_sch, mk.get("c_sqrt"))
            v.tensor_copy(out=statb[0:1, 0:R], in_=stat[0:1, 4 * R:5 * R])
            mk.set("c_statb", v.n)
            v.wait_ge(s_sch, mk.get("c_be"))
            for i in range(NT32):
                sl = i % 2
                v.wait_ge(s_yl[sl], 16 * (i // 2 + 1))
                if i >= 2:
                    v.wait_ge(s_oo[sl], 16 * (i // 2))
                v.tensor_mul(out=ost[:, sl * R:(sl + 1) * R], in0=ynt[:, sl * R:(sl + 1) * R],
                             in1=rstd_b[:])
                v.tensor_scalar(out=ost[:, sl * R:(sl + 1) * R], in0=ost[:, sl * R:(sl + 1) * R],
                                scalar1=nws[:, i:i + 1], scalar2=None, op0=ALU.mult)
                mk.set(f"c_yn{i}", v.n)

        @block.tensor
        def _(t_):
            t = CEng(t_, s_tch, auto=False)
            t.wait_ge(s_vch, mk.get("c_ones"))
            for c in range(NCH):
                t.wait_ge(s_sr, 16 * 5 * (c + 1))
                for h in range(NH):
                    u = c * NH + h
                    us = u % 2
                    if u >= 2:
                        t.wait_ge(s_sch, mk.get(f"c_ev{u - 2}"))
                    ins = t.matmul(PCs[us][0:64, 0:128],
                                   sinct[:, h * 64:(h + 1) * 64],
                                   rcs[0:64, h * 128:(h + 1) * 128],
                                   start=True, stop=True)
                    t.mark(ins)
                    mk.set(f"c_pc{u}", t.n)
                t.wait_ge(s_sch, mk.get(f"c_sq{c}"))
                for h in range(NH):
                    ins = t.matmul(pS1[0:1, c * 128:(c + 1) * 128], ones_c[0:64, :],
                                   sqc[0:64, h * 128:(h + 1) * 128],
                                   start=(h == 0), stop=(h == NH - 1))
                t.mark(ins)
                mk.set(f"c_st{c}", t.n)
            mk.set("c_mm", t.n)
            t.wait_ge(s_vch, mk.get("c_statb"))
            for ch in range(CH):
                ins = t.matmul(pC[:, ch * 512:(ch + 1) * 512], ones_r[:],
                               statb[0:1, ch * 512:(ch + 1) * 512], start=True, stop=True)
            t.mark(ins)
            mk.set("c_bc", t.n)

        @block.sync
        def _(sy):
            for i in range(NT32):
                sl = i % 2
                sy.wait_ge(s_vch, mk.get(f"c_yn{i}"))
                sy.dma_start(out=yn_o[i * 128:(i + 1) * 128, :],
                             in_=ost[:, sl * R:(sl + 1) * R]).then_inc(s_oo[sl], 16)
            for j in range(2):
                sy.wait_ge(s_oo[j], 16 * NT32 // 2)
    return nc


def host_prep_cmb(inputs, r6_zx, r7_y, r7_st, r7_rc, r7_cs):
    sta_all = np.stack([np.asarray(r7_st[i]).reshape(64, 64, 64).transpose(1, 0, 2).reshape(64, 4096)
                        for i in range(NC)])
    csa = np.stack([np.asarray(r7_cs[i]).reshape(8, 64) for i in range(NC)])
    csa_hjc = np.ascontiguousarray(csa.transpose(2, 0, 1).reshape(64, 64))
    D = np.asarray(inputs["L1_attn_D"], np.float32)
    dch = np.ascontiguousarray(np.broadcast_to(D[None, :], (64, 64)))
    nw = np.ascontiguousarray(np.asarray(inputs["L1_attn_norm_w"], np.float32).reshape(32, 128).T)
    maps = []
    for i in range(NC):
        zx = np.asarray(r6_zx[i]).reshape(80 * 128, R)
        zT = zx[:4096]
        xbT = zx[4096:5120]
        xh = np.repeat(xbT.reshape(16, 64, R), 4, axis=0).reshape(4096, R)
        def slices(M):
            Mh = M.reshape(NH, 64, NCH, 128)
            return np.ascontiguousarray(Mh.transpose(2, 1, 0, 3).reshape(NCH, 64, NH * 128))
        sel = np.zeros((64, 8), np.float32)
        if i >= 1:
            sel[:, i - 1] = 1.0
        own = np.ascontiguousarray(np.asarray(r7_cs[i]).reshape(8, 64).T)
        maps.append({
            "sta_i": sta_all, "csa_i": csa_hjc, "owncs_i": own, "sel_i": sel,
            "rc_i": np.asarray(r7_rc[i]).reshape(NCH, 64, NH * 128),
            "y_i": np.asarray(r7_y[i]).reshape(NCH, 64, NH * 128),
            "z_i": slices(zT), "xh_i": slices(xh),
            "dch_i": dch, "nw_i": nw,
        })
    return maps


def build_fin_once(mk):
    nc = bass.Bass("TRN2", target_bir_lowering=False, debug=False)
    yn_i = nc.dram_tensor("yn_i", [4096, R], BF16, kind="ExternalInput").ap()
    xn1_i = nc.dram_tensor("xn1_i", [D, R], BF16, kind="ExternalInput").ap()
    x1_i = nc.dram_tensor("x1_i", [D, R], F32, kind="ExternalInput").ap()
    ac1_i = nc.dram_tensor("ac1_i", [D, R], F32, kind="ExternalInput").ap()
    wo_t = nc.dram_tensor("wo_t", [KT, 4096, 128], F32, kind="ExternalInput").ap()
    wg8 = nc.dram_tensor("wg8", [64, D, 128], I8, kind="ExternalInput").ap()
    wu8 = nc.dram_tensor("wu8", [64, D, 128], I8, kind="ExternalInput").ap()
    wd8 = nc.dram_tensor("wd8", [KT, 8192, 128], I8, kind="ExternalInput").ap()
    sg_i = nc.dram_tensor("sg_i", [128, 64], F32, kind="ExternalInput").ap()
    su_i = nc.dram_tensor("su_i", [128, 64], F32, kind="ExternalInput").ap()
    sd_i = nc.dram_tensor("sd_i", [128, KT], F32, kind="ExternalInput").ap()
    x2_o = nc.dram_tensor("x2_o", [D, R], F32, kind="ExternalOutput").ap()
    RP = 512

    with ExitStack() as ctx:
        e = ctx.enter_context
        yns = e(nc.sbuf_tensor("yns", [128, 32 * RP], BF16))
        xn1 = e(nc.sbuf_tensor("xn1", [128, KT * R], BF16))
        itb = e(nc.sbuf_tensor("itb", [128, 32 * RP], BF16))
        acc = e(nc.sbuf_tensor("acc", [128, KT * RP], F32))
        wbuf = e(nc.sbuf_tensor("wbuf", [128, 2 * 4096], BF16))
        wgu = e(nc.sbuf_tensor("wgu", [128, 4 * D], BF16))
        w2buf = e(nc.sbuf_tensor("w2buf", [128, 2 * 4096], BF16))
        sgs = e(nc.sbuf_tensor("sgs", [128, 64], F32))
        sus = e(nc.sbuf_tensor("sus", [128, 64], F32))
        sds = e(nc.sbuf_tensor("sds", [128, KT], F32))
        g0 = e(nc.sbuf_tensor("g0", [128, 2 * RP], BF16))
        u0 = e(nc.sbuf_tensor("u0", [128, 2 * RP], BF16))
        dn0 = e(nc.sbuf_tensor("dn0", [128, 2 * RP], F32))
        xres = e(nc.sbuf_tensor("xres", [128, 2 * RP], F32))
        ares = e(nc.sbuf_tensor("ares", [128, 2 * RP], F32))
        pW = e(nc.psum_tensor("pW", [128, 512], F32))
        pW2 = e(nc.psum_tensor("pW2", [128, 512], F32))
        pG = e(nc.psum_tensor("pG", [128, 512], F32))
        pG2 = e(nc.psum_tensor("pG2", [128, 512], F32))
        pU = e(nc.psum_tensor("pU", [128, 512], F32))
        pU2 = e(nc.psum_tensor("pU2", [128, 512], F32))
        pD = e(nc.psum_tensor("pD", [128, 512], F32))
        pD2 = e(nc.psum_tensor("pD2", [128, 512], F32))
        PWs = (pW, pW2)
        PGs = (pG, pG2)
        PUs = (pU, pU2)
        PDs = (pD, pD2)

        s_ld = e(nc.semaphore("s_ld"))
        s_ly = e(nc.semaphore("s_ly"))
        s_w = [e(nc.semaphore(f"s_w{j}")) for j in range(2)]
        s_wg = [e(nc.semaphore(f"s_wg{j}")) for j in range(2)]
        s_wd = [e(nc.semaphore(f"s_wd{j}")) for j in range(2)]
        s_xa = [e(nc.semaphore(f"s_xa{j}")) for j in range(2)]
        s_oo = [e(nc.semaphore(f"s_oo{j}")) for j in range(2)]
        s_vch = e(nc.semaphore("s_vch"))
        s_sch = e(nc.semaphore("s_sch"))
        s_tch = e(nc.semaphore("s_tch"))
        block = e(nc.Block())

        @block.sync
        def _(sy):
            for i in range(KT):
                sy.dma_start(out=xn1[:, i * R:(i + 1) * R],
                             in_=xn1_i[i * 128:(i + 1) * 128, :]).then_inc(s_ld, 16)
            sy.dma_start(out=sgs[:], in_=sg_i).then_inc(s_ld, 16)
            sy.dma_start(out=sus[:], in_=su_i).then_inc(s_ld, 16)
            sy.dma_start(out=sds[:], in_=sd_i).then_inc(s_ld, 16)
            for rp in range(2):
                if rp >= 1:
                    sy.wait_ge(s_tch, mk.get(f"b_wo{KT - 1}"))
                for i in range(32):
                    sy.dma_start(out=yns[:, i * RP:(i + 1) * RP],
                                 in_=yn_i[i * 128:(i + 1) * 128, rp * RP:(rp + 1) * RP]).then_inc(s_ly, 16)

        @block.gpsimd
        def _(g):
            for rp in range(2):
                for hc in range(KT):
                    wi = rp * KT + hc
                    b = wi % 2
                    if wi >= 2:
                        g.wait_ge(s_tch, mk.get(f"b_wo{wi - 2}"))
                    g.dma_start(out=wbuf[:, b * 4096:(b + 1) * 4096].rearrange("p (t c) -> p t c", c=128),
                                in_=wo_t[hc].rearrange("(t p) c -> p t c", p=128)).then_inc(s_w[b], 16)
                for sb in range(2):
                    for nt in range(32):
                        nct = sb * 32 + nt
                        gi = rp * 64 + nct
                        b = gi % 2
                        if gi >= 2:
                            g.wait_ge(s_tch, mk.get(f"b_gu{(gi - 2) // 64}_{(gi - 2) % 64}"))
                        g.dma_start(out=wgu[:, b * D:(b + 1) * D].rearrange("p (t c) -> p t c", c=128),
                                    in_=wg8[nct].rearrange("(t p) c -> p t c", p=128)).then_inc(s_wg[b], 16)
                        g.dma_start(out=wgu[:, (2 + b) * D:(3 + b) * D].rearrange("p (t c) -> p t c", c=128),
                                    in_=wu8[nct].rearrange("(t p) c -> p t c", p=128)).then_inc(s_wg[b], 16)
                    for hc in range(KT):
                        di = (rp * 2 + sb) * KT + hc
                        b = di % 2
                        if di >= 2:
                            g.wait_ge(s_tch, mk.get(f"b_dn{di - 2}"))
                        g.dma_start(out=w2buf[:, b * 4096:(b + 1) * 4096].rearrange("p (t c) -> p t c", c=128),
                                    in_=wd8[hc][sb * 4096:(sb + 1) * 4096, :].rearrange("(t p) c -> p t c", p=128)
                                    ).then_inc(s_wd[b], 16)
                for hc in range(KT):
                    xi = rp * KT + hc
                    sl = xi % 2
                    if xi >= 2:
                        g.wait_ge(s_oo[sl], 16 * (xi // 2))
                    g.dma_start(out=xres[:, sl * RP:(sl + 1) * RP],
                                in_=x1_i[hc * 128:(hc + 1) * 128, rp * RP:(rp + 1) * RP]).then_inc(s_xa[sl], 16)
                    g.dma_start(out=ares[:, sl * RP:(sl + 1) * RP],
                                in_=ac1_i[hc * 128:(hc + 1) * 128, rp * RP:(rp + 1) * RP]).then_inc(s_xa[sl], 16)

        @block.tensor
        def _(t_):
            t = CEng(t_, s_tch, auto=False)
            t.wait_ge(s_ld, 16 * (KT + 3))
            for rp in range(2):
                t.wait_ge(s_ly, 16 * 32 * (rp + 1))
                for hc in range(KT):
                    wi = rp * KT + hc
                    b = wi % 2
                    sl = wi % 2
                    t.wait_ge(s_w[b], 16 * (wi // 2 + 1))
                    if wi >= 2:
                        t.wait_ge(s_sch, mk.get(f"b_woe{wi - 2}"))
                    for k in range(32):
                        ins = t.matmul(PWs[sl][:], wbuf[:, b * 4096 + k * 128:b * 4096 + (k + 1) * 128],
                                       yns[:, k * RP:(k + 1) * RP],
                                       start=(k == 0), stop=(k == 31))
                    t.mark(ins)
                    mk.set(f"b_wo{wi}", t.n)
                for sb in range(2):
                    for nt in range(32):
                        nct = sb * 32 + nt
                        gi = rp * 64 + nct
                        b = gi % 2
                        sl = gi % 2
                        t.wait_ge(s_wg[b], 16 * 2 * (gi // 2 + 1))
                        if gi >= 2:
                            t.wait_ge(s_sch, mk.get(f"b_gue{(gi - 2) // 64}_{(gi - 2) % 64}"))
                        for k in range(KT):
                            t.matmul(PGs[sl][:], wgu[:, b * D + k * 128:b * D + (k + 1) * 128],
                                     xn1[:, k * R + rp * RP:k * R + (rp + 1) * RP],
                                     start=(k == 0), stop=(k == KT - 1))
                        for k in range(KT):
                            ins = t.matmul(PUs[sl][:], wgu[:, (2 + b) * D + k * 128:(2 + b) * D + (k + 1) * 128],
                                           xn1[:, k * R + rp * RP:k * R + (rp + 1) * RP],
                                           start=(k == 0), stop=(k == KT - 1))
                        t.mark(ins)
                        mk.set(f"b_gu{rp}_{nct}", t.n)
                    for hc in range(KT):
                        di = (rp * 2 + sb) * KT + hc
                        b = di % 2
                        sl = di % 2
                        t.wait_ge(s_wd[b], 16 * (di // 2 + 1))
                        t.wait_ge(s_vch, mk.get(f"b_it{rp}_{sb * 32 + 31}"))
                        if di >= 2:
                            t.wait_ge(s_sch, mk.get(f"b_dne{di - 2}"))
                        for k in range(32):
                            ins = t.matmul(PDs[sl][:], w2buf[:, b * 4096 + k * 128:b * 4096 + (k + 1) * 128],
                                           itb[:, k * RP:(k + 1) * RP],
                                           start=(k == 0), stop=(k == 31))
                        t.mark(ins)
                        mk.set(f"b_dn{di}", t.n)

        @block.scalar
        def _(s_):
            s = CEng(s_, s_sch)
            s.wait_ge(s_ld, 16 * (KT + 3))
            for rp in range(2):
                for hc in range(KT):
                    wi = rp * KT + hc
                    sl = wi % 2
                    s.wait_ge(s_tch, mk.get(f"b_wo{wi}"))
                    if rp >= 1:
                        s.wait_ge(s_vch, mk.get(f"b_x{rp - 1}_{hc}"))
                    s.activation(out=acc[:, hc * RP:(hc + 1) * RP], in_=PWs[sl][:],
                                 func=AF.Identity)
                    mk.set(f"b_woe{wi}", s.n)
                for sb in range(2):
                    for nt in range(32):
                        nct = sb * 32 + nt
                        gi = rp * 64 + nct
                        sl = gi % 2
                        s.wait_ge(s_tch, mk.get(f"b_gu{rp}_{nct}"))
                        if gi >= 2:
                            s.wait_ge(s_vch, mk.get(f"b_it{(gi - 2) // 64}_{(gi - 2) % 64}"))
                        s.activation(out=g0[:, sl * RP:(sl + 1) * RP], in_=PGs[sl][:],
                                     func=AF.Silu, scale=sgs[:, nct:nct + 1])
                        s.activation(out=u0[:, sl * RP:(sl + 1) * RP], in_=PUs[sl][:],
                                     func=AF.Identity, scale=sus[:, nct:nct + 1])
                        mk.set(f"b_gue{rp}_{nct}", s.n)
                    for hc in range(KT):
                        di = (rp * 2 + sb) * KT + hc
                        sl = di % 2
                        s.wait_ge(s_tch, mk.get(f"b_dn{di}"))
                        if di >= 2:
                            s.wait_ge(s_vch, mk.get(f"b_dv{di - 2}"))
                        s.activation(out=dn0[:, sl * RP:(sl + 1) * RP], in_=PDs[sl][:],
                                     func=AF.Identity, scale=sds[:, hc:hc + 1])
                        mk.set(f"b_dne{di}", s.n)

        @block.vector
        def _(v_):
            v = CEng(v_, s_vch)
            for rp in range(2):
                for sb in range(2):
                    for nt in range(32):
                        nct = sb * 32 + nt
                        gi = rp * 64 + nct
                        sl = gi % 2
                        v.wait_ge(s_sch, mk.get(f"b_gue{rp}_{nct}"))
                        if rp + sb > 0:
                            prev_last_di = (rp * 2 + sb - 1 + 1) * KT - 1
                            v.wait_ge(s_tch, mk.get(f"b_dn{prev_last_di}"))
                        v.tensor_mul(out=itb[:, nt * RP:(nt + 1) * RP],
                                     in0=g0[:, sl * RP:(sl + 1) * RP], in1=u0[:, sl * RP:(sl + 1) * RP])
                        mk.set(f"b_it{rp}_{nct}", v.n)
                    for hc in range(KT):
                        di = (rp * 2 + sb) * KT + hc
                        sl = di % 2
                        v.wait_ge(s_sch, mk.get(f"b_dne{di}"))
                        v.wait_ge(s_sch, mk.get(f"b_woe{rp * KT + hc}"))
                        v.tensor_add(out=acc[:, hc * RP:(hc + 1) * RP],
                                     in0=acc[:, hc * RP:(hc + 1) * RP], in1=dn0[:, sl * RP:(sl + 1) * RP])
                        mk.set(f"b_dv{di}", v.n)
                        if sb == 1:
                            mk.set(f"b_acv_pre{rp}_{hc}", v.n)
                for hc in range(KT):
                    xi = rp * KT + hc
                    sl = xi % 2
                    v.wait_ge(s_xa[sl], 32 * (xi // 2 + 1))
                    v.tensor_add(out=xres[:, sl * RP:(sl + 1) * RP],
                                 in0=acc[:, hc * RP:(hc + 1) * RP], in1=xres[:, sl * RP:(sl + 1) * RP])
                    v.tensor_add(out=xres[:, sl * RP:(sl + 1) * RP],
                                 in0=xres[:, sl * RP:(sl + 1) * RP], in1=ares[:, sl * RP:(sl + 1) * RP])
                    mk.set(f"b_x{rp}_{hc}", v.n)
                    mk.set(f"b_acv{rp * KT + hc}", v.n)

        @block.sync
        def _(sy):
            for rp in range(2):
                for hc in range(KT):
                    xi = rp * KT + hc
                    sl = xi % 2
                    sy.wait_ge(s_vch, mk.get(f"b_x{rp}_{hc}"))
                    sy.dma_start(out=x2_o[hc * 128:(hc + 1) * 128, rp * RP:(rp + 1) * RP],
                                 in_=xres[:, sl * RP:(sl + 1) * RP]).then_inc(s_oo[sl], 16)
            for j in range(2):
                sy.wait_ge(s_oo[j], 16 * KT)
    return nc


def host_prep_fin(inputs, yn_shards, xn1_shards, x1_shards, ac1_shards):
    wo_tl = tile_w(np.ascontiguousarray(np.asarray(inputs["L1_attn_W_out"], np.float32)))
    wg8 = tile_w(np.ascontiguousarray(np.asarray(inputs["L1_ffn_w_gate"])))
    wu8 = tile_w(np.ascontiguousarray(np.asarray(inputs["L1_ffn_w_up"])))
    wd8 = tile_w(np.ascontiguousarray(np.asarray(inputs["L1_ffn_w_down"])))
    sg = np.ascontiguousarray(np.asarray(inputs["L1_ffn_s_gate"], np.float32).reshape(64, 128).T)
    su = np.ascontiguousarray(np.asarray(inputs["L1_ffn_s_up"], np.float32).reshape(64, 128).T)
    sd = np.ascontiguousarray(np.asarray(inputs["L1_ffn_s_down"], np.float32).reshape(KT, 128).T)
    maps = []
    for i in range(NC):
        maps.append({
            "yn_i": np.ascontiguousarray(yn_shards[i]),
            "xn1_i": np.ascontiguousarray(xn1_shards[i]),
            "x1_i": np.ascontiguousarray(x1_shards[i]),
            "ac1_i": np.ascontiguousarray(ac1_shards[i]),
            "wo_t": wo_tl, "wg8": wg8, "wu8": wu8, "wd8": wd8,
            "sg_i": sg, "su_i": su, "sd_i": sd,
        })
    return maps

